# version 14
# baseline (speedup 1.0000x reference)
import numpy as np
import concourse.bass as bass
import concourse.mybir as mybir
from concourse.bass_utils import run_bass_kernel_spmd

F32 = mybir.dt.float32
BF16 = mybir.dt.bfloat16
AF = mybir.ActivationFunctionType
ALU = mybir.AluOpType
AX = mybir.AxisListType

T = 2048
D = 1024
DC = 8
NTT = 16
NMT = 4
H = 16
HPAIR = 8
NK = 8192
NSLAB = 8
NKC = 8
NE = 8
NQT = 4
NJ = NSLAB * NKC * NQT
LN_EPS = 1e-5
NRM_EPS = 1e-12

ENGY, ENGT, ENGV, ENGA, ENGG = "sync", "tensor", "vector", "scalar", "gpsimd"


class Prog:
    def __init__(self, nc, stack):
        self.nc = nc
        self.sem = {}
        self.ctr = {}
        for n in ("mm", "dv", "ac", "gp", "ld", "gd", "cc"):
            self.sem[n] = stack.enter_context(nc.semaphore(n))
            self.ctr[n] = 0
        self.lastw = {}

    def inc(self, name, instr, n=1):
        instr.then_inc(self.sem[name], n)
        self.ctr[name] += n
        return self.ctr[name]

    def wait(self, eng, engname, sname, val):
        if val is None or val <= 0:
            return
        key = (engname, sname)
        if self.lastw.get(key, -1) >= val:
            return
        eng.wait_ge(self.sem[sname], val)
        self.lastw[key] = val


def fap(base_ap, col_off, dims):
    return bass.AP(base_ap.tensor, base_ap.offset + col_off,
                   [list(base_ap.ap[0])] + [list(d) for d in dims])


def dap(dram_ap, el_off, dims):
    return bass.AP(dram_ap.tensor, dram_ap.offset + el_off, [list(d) for d in dims])


def build(stage="full", dbg=False):
    import contextlib
    nc = bass.Bass("TRN2", target_bir_lowering=False)

    x_d = nc.dram_tensor("x", [T, D], F32, kind="ExternalInput").ap()
    wq_d = nc.dram_tensor("wq", [128, 2048], F32, kind="ExternalInput").ap()
    wk_d = nc.dram_tensor("wk", [128, 2048], F32, kind="ExternalInput").ap()
    wv_d = nc.dram_tensor("wv", [128, 8192], F32, kind="ExternalInput").ap()
    wo_d = nc.dram_tensor("wo", [128, 8192], F32, kind="ExternalInput").ap()
    gt_d = nc.dram_tensor("gt", [128, 64], F32, kind="ExternalInput").ap()
    ew_d = nc.dram_tensor("ew", [NE, DC, 128, 1024], F32, kind="ExternalInput").ap()
    out_d = nc.dram_tensor("out", [T, D], F32, kind="ExternalOutput").ap()

    qhl_d = nc.dram_tensor("qhl_d", [T, H * 64], BF16).ap()
    khl_d = nc.dram_tensor("khl_d", [T, H * 64], BF16).ap()
    vht_d = nc.dram_tensor("vht_d", [T, D], F32).ap()
    dbgw_d = (nc.dram_tensor("dbgw", [8, T], F32, kind="ExternalOutput").ap()
              if dbg else None)
    dbgl_d = (nc.dram_tensor("dbgl", [128, NTT * 8], F32, kind="ExternalOutput").ap()
              if dbg else None)
    dbgf_d = (nc.dram_tensor("dbgf", [128, NTT * 8], F32, kind="ExternalOutput").ap()
              if dbg else None)
    khl_all = [nc.dram_tensor(f"khl_all{c}", [NK // 2, H * 64], BF16).ap() for c in range(2)]
    vht_all = [nc.dram_tensor(f"vht_all{c}", [NK // 2, D], F32).ap() for c in range(2)]
    khl_dbg = (nc.dram_tensor("khl_dbg", [NK, H * 64], BF16, kind="ExternalOutput").ap()
               if dbg else None)
    att_dbg = (nc.dram_tensor("att_dbg", [128, DC * T], F32, kind="ExternalOutput").ap()
               if dbg else None)
    qhl_dbg = (nc.dram_tensor("qhl_dbg", [T, H * 64], BF16, kind="ExternalOutput").ap()
               if dbg else None)
    dbgq_d = (nc.dram_tensor("dbgq", [128, 2048], F32, kind="ExternalOutput").ap()
              if dbg else None)
    dbgs_d = (nc.dram_tensor("dbgs", [128, 1536], F32, kind="ExternalOutput").ap()
              if dbg else None)
    vht_dbg = (nc.dram_tensor("vht_dbg", [NK, D], F32, kind="ExternalOutput").ap()
               if dbg else None)

    idn_c = nc.inline_tensor(np.eye(128, dtype=np.float32), "idn_c")
    ones1_c = nc.inline_tensor(np.ones((1, 128), dtype=np.float32), "ones1_c")
    onesc_c = nc.inline_tensor(np.ones((128, 1), dtype=np.float32), "onesc_c")
    sel2_np = np.zeros((2, 128), dtype=np.float32)
    sel2_np[0, 0:64] = 1.0
    sel2_np[1, 64:128] = 1.0
    sel2_c = nc.inline_tensor(sel2_np, "sel2_c")
    blk2_np = np.zeros((128, 2), dtype=np.float32)
    blk2_np[0:64, 0] = 1.0
    blk2_np[64:128, 1] = 1.0
    blk2_c = nc.inline_tensor(blk2_np, "blk2_c")
    sels_np = np.zeros((8, 8 * 128), dtype=np.float32)
    for e in range(8):
        sels_np[e, e * 128:(e + 1) * 128] = 1.0
    sels_c = nc.inline_tensor(sels_np, "sels_c")
    tie_np = np.zeros((128, 64), dtype=np.float32)
    for e in range(8):
        for e2 in range(e):
            tie_np[:, e * 8 + e2] = 1.0
    tie_c = nc.inline_tensor(tie_np, "tie_c")

    XT = nc.alloc_sbuf_tensor("XT", [128, DC * T], F32).ap()
    YTH = nc.alloc_sbuf_tensor("YTH", [128, DC * T], BF16).ap()
    YTL = nc.alloc_sbuf_tensor("YTL", [128, DC * T], BF16).ap()
    IDN = nc.alloc_sbuf_tensor("IDN", [128, 128], F32).ap()
    ONES1 = nc.alloc_sbuf_tensor("ONES1", [1, 128], F32).ap()
    ONESC = nc.alloc_sbuf_tensor("ONESC", [128, 1], F32).ap()
    SEL2 = nc.alloc_sbuf_tensor("SEL2", [2, 128], F32).ap()
    BLK2 = nc.alloc_sbuf_tensor("BLK2", [128, 2], F32).ap()
    SELS = nc.alloc_sbuf_tensor("SELS", [8, 8 * 128], F32).ap()
    TIE = nc.alloc_sbuf_tensor("TIE", [128, 64], F32).ap()
    LOG = nc.alloc_sbuf_tensor("LOG", [128, NTT * 8], F32).ap()
    WFT = nc.alloc_sbuf_tensor("WFT", [128, NTT * 8], F32).ap()
    SCR = nc.alloc_sbuf_tensor("SCR", [128, 15424], F32).ap()
    BSC = nc.alloc_sbuf_tensor("BSC", [128, 6144], BF16).ap()
    PS = nc.alloc_psum_tensor("PS", [128, 4096], F32).ap()

    stack = contextlib.ExitStack()
    with stack:
        P = Prog(nc, stack)
        blk = stack.enter_context(nc.Block())

        def W(eng, ename, **kw):
            for sname, val in kw.items():
                P.wait(eng, ename, sname, val)

        ld_x = [0] * NTT
        tin_v = [0] * NTT

        @blk.sync
        def _(y):
            for cst, dst in ((idn_c, IDN), (ones1_c, ONES1), (onesc_c, ONESC), (sel2_c, SEL2),
                             (blk2_c, BLK2), (sels_c, SELS), (tie_c, TIE)):
                P.inc("ld", y.dma_start(out=dst, in_=cst.ap()), 16)

        for t in range(NTT):
            xst = SCR[:, (t % 2) * 1024:(t % 2) * 1024 + 1024]

            @blk.sync
            def _(y, t=t, xst=xst):
                if t >= 2:
                    W(y, ENGY, dv=tin_v[t - 2])
                ld_x[t] = P.inc("ld", y.dma_start(out=xst, in_=x_d[t * 128:(t + 1) * 128, :]), 16)

            @blk.tensor
            def _(te, t=t, xst=xst):
                W(te, ENGT, ld=ld_x[t])
                if t >= 2:
                    W(te, ENGT, dv=tin_v[t - 2])
                g = (t % 2) * 2
                for c in range(DC):
                    P.inc("mm", te.transpose(PS[:, g * 512 + c * 128: g * 512 + c * 128 + 128],
                                             xst[:, c * 128:c * 128 + 128], IDN))

            @blk.vector
            def _(v, t=t):
                W(v, ENGV, mm=P.ctr["mm"])
                g = (t % 2) * 2
                src = fap(PS, g * 512, [[128, 8], [1, 128]])
                dst = fap(XT, t * 128, [[T, 8], [1, 128]])
                tin_v[t] = P.inc("dv", v.tensor_copy(dst, src))


        def emit_ln():
            SQ = [SCR[:, 0:512], SCR[:, 512:1024]]
            MT = SCR[:, 3072:5120]
            RT = SCR[:, 5120:7168]
            s_r = SCR[0:1, 1024:1024 + T]
            ss_r = SCR[0:1, 3072:3072 + T]
            t_r = SCR[0:1, 5120:5120 + T]
            r_r = t_r
            mr_r = s_r
            start_dv = P.ctr["dv"]
            start_ac = P.ctr["ac"]
            start_mm = P.ctr["mm"]
            sq_a = [0] * (DC * NMT)
            last_ss_mm = [0, 0]

            @blk.tensor
            def _(te):
                W(te, ENGT, dv=start_dv, ac=start_ac)
                for cci in range(DC):
                    for nq in range(NMT):
                        P.inc("mm", te.matmul(PS[0:1, nq * 512:nq * 512 + 512], ONESC,
                                              XT[:, cci * T + nq * 512: cci * T + nq * 512 + 512],
                                              start=(cci == 0), stop=(cci == DC - 1)))

            for idx in range(DC * NMT):
                cci, nq = idx // NMT, idx % NMT

                @blk.scalar
                def _(a, idx=idx, cci=cci, nq=nq):
                    if idx >= 2:
                        W(a, ENGA, mm=last_ss_mm[idx % 2])
                    else:
                        W(a, ENGA, dv=start_dv, mm=start_mm)
                    sq_a[idx] = P.inc("ac", a.activation(
                        out=SQ[idx % 2], in_=XT[:, cci * T + nq * 512: cci * T + nq * 512 + 512],
                        func=AF.Square))

                @blk.tensor
                def _(te, idx=idx, cci=cci, nq=nq):
                    W(te, ENGT, ac=sq_a[idx])
                    last_ss_mm[idx % 2] = P.inc("mm", te.matmul(
                        PS[0:1, 2048 + nq * 512: 2048 + nq * 512 + 512], ONESC, SQ[idx % 2],
                        start=(cci == 0), stop=(cci == DC - 1)))

            @blk.vector
            def _(v):
                W(v, ENGV, mm=P.ctr["mm"])
                P.inc("dv", v.tensor_copy(s_r, PS[0:1, 0:T]))
                P.inc("dv", v.tensor_copy(ss_r, PS[0:1, 2048:2048 + T]))
                P.inc("dv", v.tensor_scalar(s_r, s_r, 1.0 / D, None, ALU.mult))
                P.inc("dv", v.tensor_scalar(ss_r, ss_r, 1.0 / D, None, ALU.mult))
                P.inc("dv", v.tensor_tensor(t_r, s_r, s_r, ALU.mult))
                P.inc("dv", v.tensor_tensor(ss_r, ss_r, t_r, ALU.subtract))
                P.inc("dv", v.tensor_scalar(ss_r, ss_r, LN_EPS, None, ALU.add))

            @blk.scalar
            def _(a):
                W(a, ENGA, dv=P.ctr["dv"])
                P.inc("ac", a.activation(out=t_r, in_=ss_r, func=AF.Sqrt))

            @blk.vector
            def _(v):
                W(v, ENGV, ac=P.ctr["ac"])
                P.inc("dv", v.reciprocal(r_r, t_r))
                P.inc("dv", v.tensor_tensor(mr_r, s_r, r_r, ALU.mult))

            bc_cp = [0] * (2 * NMT)
            for bi, (row, dst) in enumerate(((r_r, RT), (mr_r, MT))):
                for nq in range(NMT):
                    k = bi * NMT + nq
                    bc_mm = 0

                    @blk.tensor
                    def _(te, row=row, nq=nq, k=k):
                        W(te, ENGT, dv=P.ctr["dv"] if k == 0 else (bc_cp[k - 2] if k >= 2 else 0))
                        nonlocal bc_mm
                        bc_mm = P.inc("mm", te.matmul(
                            PS[:, (k % 2) * 512:(k % 2) * 512 + 512], ONES1,
                            row[0:1, nq * 512:nq * 512 + 512], start=True, stop=True))

                    @blk.vector
                    def _(v, dst=dst, nq=nq, k=k):
                        W(v, ENGV, mm=bc_mm)
                        bc_cp[k] = P.inc("dv", v.tensor_copy(
                            dst[:, nq * 512:nq * 512 + 512],
                            PS[:, (k % 2) * 512:(k % 2) * 512 + 512]))

            TMPY = SCR[:, 7232:7232 + T]
            Y2 = SCR[:, 9280:9280 + T]
            TMPYg = SCR[:, 12800:12800 + 1024]
            Y2g = SCR[:, 13824:13824 + 1024]
            ln_done_dv = 0
            ln_done_gp = 0

            GP_SPLIT = True

            @blk.vector
            def _(v):
                nonlocal ln_done_dv
                for cci in range(DC // 2 if GP_SPLIT else DC):
                    P.inc("dv", v.tensor_tensor(TMPY, XT[:, cci * T:(cci + 1) * T], RT, ALU.mult))
                    P.inc("dv", v.tensor_tensor(Y2, TMPY, MT, ALU.subtract))
                    P.inc("dv", v.tensor_copy(YTH[:, cci * T:(cci + 1) * T], Y2))
                    ln_done_dv = P.inc("dv", v.tensor_tensor(
                        YTL[:, cci * T:(cci + 1) * T], Y2,
                        YTH[:, cci * T:(cci + 1) * T], ALU.subtract))

            @blk.gpsimd
            def _(g):
                nonlocal ln_done_gp
                if not GP_SPLIT:
                    return
                W(g, ENGG, dv=P.ctr["dv"] - 4 * (DC // 2), mm=P.ctr["mm"])
                for cci in range(DC // 2, DC):
                    for half in range(2):
                        sl0 = cci * T + half * 1024
                        P.inc("gp", g.tensor_tensor(TMPYg, XT[:, sl0:sl0 + 1024],
                                                    RT[:, half * 1024:half * 1024 + 1024], ALU.mult))
                        P.inc("gp", g.tensor_tensor(Y2g, TMPYg,
                                                    MT[:, half * 1024:half * 1024 + 1024], ALU.subtract))
                        P.inc("gp", g.tensor_copy(YTH[:, sl0:sl0 + 1024], Y2g))
                        ln_done_gp = P.inc("gp", g.tensor_tensor(
                            YTL[:, sl0:sl0 + 1024], Y2g,
                            YTH[:, sl0:sl0 + 1024], ALU.subtract))
            return ln_done_gp

        def emit_gating():
            GTS = SCR[:, 7168:7232]
            GTB = BSC[:, 6016:6144]
            GS = SCR[:, 7232:8256]
            GS2 = SCR[:, 8256:9280]
            CNT = SCR[:, 9280:9408]
            SELB = SCR[:, 9408:9536]
            ELB = SCR[:, 9552:9680]
            NVM = SCR[:, 9536:9552]
            SM = SCR[:, 9680:9696]
            RSM = SCR[:, 9696:9712]
            W8 = SCR[0:8, 9728:9728 + T]
            gts_ld = 0

            @blk.sync
            def _(y):
                nonlocal gts_ld
                W(y, ENGY, dv=P.ctr["dv"], mm=P.ctr["mm"], ac=P.ctr["ac"])
                gts_ld = P.inc("ld", y.dma_start(out=GTS, in_=gt_d), 16)

            gts_dv = 0

            @blk.vector
            def _(v):
                nonlocal gts_dv
                W(v, ENGV, ld=gts_ld)
                P.inc("dv", v.tensor_copy(GTB[:, 0:64], GTS))
                gts_dv = P.inc("dv", v.tensor_tensor(GTB[:, 64:128], GTS, GTB[:, 0:64],
                                                     ALU.subtract))

            lg_mm = [0] * NTT
            cp_dv = [0] * NTT
            for t in range(NTT):
                @blk.tensor
                def _(te, t=t):
                    W(te, ENGT, dv=gts_dv)
                    if t == 0:
                        W(te, ENGT, dv=P.ctr["dv"], gp=ln_gp[0])
                    if t >= 2:
                        W(te, ENGT, dv=cp_dv[t - 2])
                    for cci in range(DC):
                        for (sta, mov) in ((YTH, GTB[:, 0:64]), (YTH, GTB[:, 64:128]),
                                           (YTL, GTB[:, 0:64])):
                            P.inc("mm", te.matmul(
                                PS[:, (t % 2) * 512:(t % 2) * 512 + 8],
                                sta[:, cci * T + t * 128: cci * T + t * 128 + 128],
                                mov[:, cci * 8:cci * 8 + 8],
                                start=(cci == 0 and sta is YTH and mov.offset == GTB.offset),
                                stop=(cci == DC - 1 and sta is YTL)))
                    lg_mm[t] = P.ctr["mm"]

                @blk.vector
                def _(v, t=t):
                    W(v, ENGV, mm=lg_mm[t])
                    cp_dv[t] = P.inc("dv", v.tensor_copy(LOG[:, t * 8:t * 8 + 8],
                                                         PS[:, (t % 2) * 512:(t % 2) * 512 + 8]))

            @blk.vector
            def _(v):
                l_ee = fap(LOG, 0, [[8, NTT], [0, 8], [1, 8]])
                l_e = fap(LOG, 0, [[8, NTT], [1, 8], [0, 8]])
                gsv = fap(GS, 0, [[64, NTT], [8, 8], [1, 8]])
                gs2v = fap(GS2, 0, [[64, NTT], [8, 8], [1, 8]])
                tiev = fap(TIE, 0, [[0, NTT], [8, 8], [1, 8]])
                P.inc("dv", v.tensor_tensor(gsv, l_ee, l_e, ALU.is_gt))
                P.inc("dv", v.tensor_tensor(gs2v, l_ee, l_e, ALU.is_equal))
                P.inc("dv", v.tensor_tensor(gs2v, gs2v, tiev, ALU.mult))
                P.inc("dv", v.tensor_tensor(gsv, gsv, gs2v, ALU.add))
                P.inc("dv", v.tensor_reduce(fap(CNT, 0, [[8, NTT], [1, 8], [0, 1]]), gsv,
                                            AX.X, ALU.add))
                P.inc("dv", v.reduce_max(fap(NVM, 0, [[1, NTT], [0, 1]]),
                                         fap(LOG, 0, [[8, NTT], [1, 8]]), AX.X, negate=True))
                P.inc("dv", v.tensor_copy(GS2[:, 0:128], GS[:, 0:128]))
                P.inc("dv", v.tensor_single_scalar(SELB, CNT, 2.0, ALU.is_lt))
                P.inc("dv", v.tensor_tensor(fap(ELB, 0, [[8, NTT], [1, 8]]),
                                            fap(LOG, 0, [[8, NTT], [1, 8]]),
                                            fap(NVM, 0, [[1, NTT], [0, 8]]), ALU.add))

            @blk.scalar
            def _(a):
                W(a, ENGA, dv=P.ctr["dv"])
                P.inc("ac", a.activation(out=ELB, in_=ELB, func=AF.Exp))

            @blk.vector
            def _(v):
                W(v, ENGV, ac=P.ctr["ac"])
                P.inc("dv", v.tensor_tensor(ELB, ELB, SELB, ALU.mult))
                P.inc("dv", v.reduce_sum(fap(SM, 0, [[1, NTT], [0, 1]]),
                                         fap(ELB, 0, [[8, NTT], [1, 8]]), AX.X))
                P.inc("dv", v.tensor_copy(GS2[:, 0:128], GS[:, 0:128]))
                P.inc("dv", v.tensor_copy(GS2[:, 128:256], GS[:, 128:256]))
                P.inc("dv", v.reciprocal(RSM, SM))
                P.inc("dv", v.tensor_copy(GS2[:, 256:384], GS[:, 256:384]))
                P.inc("dv", v.scalar_tensor_tensor(fap(WFT, 0, [[8, NTT], [1, 8]]),
                                                   fap(ELB, 0, [[8, NTT], [1, 8]]), 0.5,
                                                   fap(RSM, 0, [[1, NTT], [0, 8]]),
                                                   ALU.mult, ALU.mult))

            tr_dv = [0] * NTT
            for t in range(NTT):
                tr_mm = 0

                @blk.tensor
                def _(te, t=t):
                    W(te, ENGT, dv=P.ctr["dv"])
                    if t >= 2:
                        W(te, ENGT, dv=tr_dv[t - 2])
                    nonlocal tr_mm
                    tr_mm = P.inc("mm", te.transpose(
                        PS[0:8, 1024 + (t % 2) * 512:1024 + (t % 2) * 512 + 128],
                        WFT[:, t * 8:t * 8 + 8], IDN))

                @blk.vector
                def _(v, t=t, tr_mm_get=lambda: tr_mm):
                    W(v, ENGV, mm=tr_mm_get())
                    tr_dv[t] = P.inc("dv", v.tensor_copy(
                        W8[0:8, t * 128:t * 128 + 128],
                        PS[0:8, 1024 + (t % 2) * 512:1024 + (t % 2) * 512 + 128]))
            return W8

        last_xt_dv = {}

        def emit_ffn(W8):
            WB = SCR[:, 0:2048]
            EWF = [SCR[:, 2048:2048 + 1024], SCR[:, 3072:3072 + 1024]]
            EWH = [BSC[:, 0:1024], BSC[:, 1024:2048]]
            EWL = [BSC[:, 2048:3072], BSC[:, 3072:4096]]
            TMP = [SCR[:, 4096:4096 + 512], SCR[:, 4608:4608 + 512]]
            nidx = NE * DC
            ld_s = [0] * nidx
            split_dv = [0] * nidx
            use_mm = [0] * nidx
            tmp_dv = {}
            wb_dv = [0] * NE
            first = [True]

            def slab_dma(y, idx):
                e, dd = idx // DC, idx % DC
                if first[0]:
                    W(y, ENGY, mm=P.ctr["mm"], dv=P.ctr["dv"], ac=P.ctr["ac"],
                      gp=ln_gp[0])
                    first[0] = False
                if idx >= 2:
                    W(y, ENGY, dv=split_dv[idx - 2])
                ld_s[idx] = P.inc("ld", y.dma_start(
                    out=EWF[idx % 2], in_=ew_d[e, dd].opt()), 16)

            mm_at_entry = P.ctr["mm"]

            def slab_split(idx):
                @blk.vector
                def _(v):
                    W(v, ENGV, ld=ld_s[idx])
                    if idx >= 2:
                        W(v, ENGV, mm=use_mm[idx - 2])
                    else:
                        W(v, ENGV, mm=mm_at_entry)
                    P.inc("dv", v.tensor_copy(EWH[idx % 2], EWF[idx % 2]))
                    split_dv[idx] = P.inc("dv", v.tensor_tensor(
                        EWL[idx % 2], EWF[idx % 2], EWH[idx % 2], ALU.subtract))

            @blk.sync
            def _(y):
                slab_dma(y, 0)
                slab_dma(y, 1)

            slab_split(0)
            slab_split(1)

            for e in range(NE):
                wb_cp = [0] * NMT
                for nq in range(NMT):
                    bc_mm = 0

                    @blk.tensor
                    def _(te, e=e, nq=nq):
                        if nq == 0:
                            W(te, ENGT, dv=wb_dv[e - 1] if e >= 1 else P.ctr["dv"])
                        if nq >= 2:
                            W(te, ENGT, dv=wb_cp[nq - 2])
                        nonlocal bc_mm
                        bc_mm = P.inc("mm", te.matmul(
                            PS[:, (2 + nq % 2) * 512:(2 + nq % 2) * 512 + 512],
                            SELS[:, e * 128:e * 128 + 128],
                            W8[0:8, nq * 512:nq * 512 + 512], start=True, stop=True))

                    @blk.vector
                    def _(v, e=e, nq=nq):
                        W(v, ENGV, mm=bc_mm)
                        wb_cp[nq] = P.inc("dv", v.tensor_copy(
                            WB[:, nq * 512:nq * 512 + 512],
                            PS[:, (2 + nq % 2) * 512:(2 + nq % 2) * 512 + 512]))
                        wb_dv[e] = wb_cp[nq]

                for dd in range(DC):
                    idx = e * DC + dd

                    for mt in range(NMT):
                        @blk.tensor
                        def _(te, idx=idx, e=e, mt=mt):
                            W(te, ENGT, dv=max(split_dv[idx], wb_dv[e]), gp=ln_gp[0])
                            prev = tmp_dv.get((idx, mt - 2), tmp_dv.get((idx - 1, mt + 2)))
                            if prev:
                                W(te, ENGT, dv=prev)
                            for cci in range(DC):
                                for pi, (sta, mov) in enumerate(((EWH, YTH), (EWH, YTL), (EWL, YTH))):
                                    P.inc("mm", te.matmul(
                                        PS[:, (mt % 2) * 512:(mt % 2) * 512 + 512],
                                        sta[idx % 2][:, cci * 128:cci * 128 + 128],
                                        mov[:, cci * T + mt * 512: cci * T + mt * 512 + 512],
                                        start=(cci == 0 and pi == 0),
                                        stop=(cci == DC - 1 and pi == 2)))
                            use_mm[idx] = P.ctr["mm"]

                        @blk.vector
                        def _(v, idx=idx, dd=dd, mt=mt):
                            W(v, ENGV, mm=use_mm[idx])
                            P.inc("dv", v.tensor_tensor(
                                TMP[mt % 2], PS[:, (mt % 2) * 512:(mt % 2) * 512 + 512],
                                WB[:, mt * 512:mt * 512 + 512], ALU.mult))
                            xs = XT[:, dd * T + mt * 512: dd * T + mt * 512 + 512]
                            tmp_dv[(idx, mt)] = P.inc("dv", v.tensor_tensor(xs, xs, TMP[mt % 2], ALU.add))
                            last_xt_dv[(idx, mt)] = tmp_dv[(idx, mt)]

                    @blk.sync
                    def _(y, idx=idx):
                        if idx + 2 < nidx:
                            slab_dma(y, idx + 2)

                    if idx + 2 < nidx:
                        slab_split(idx + 2)

        def emit_qkv():
            WQS = SCR[:, 0:2048]
            WKS = SCR[:, 2048:4096]
            WVS = SCR[:, 4096:12288]
            QKV = SCR[:, 12288:13824]
            SQK = SCR[:, 13824:14336]
            NRM = SCR[:, 14336:14384]
            QHL = BSC[:, 0:1024]
            KHL = BSC[:, 1024:2048]
            ldw = 0

            @blk.sync
            def _(y):
                nonlocal ldw
                W(y, ENGY, mm=P.ctr["mm"], dv=P.ctr["dv"], ac=P.ctr["ac"], gp=ln_gp[0])
                P.inc("ld", y.dma_start(out=WQS, in_=wq_d), 16)
                P.inc("ld", y.dma_start(out=WKS, in_=wk_d), 16)
                ldw = P.inc("ld", y.dma_start(out=WVS, in_=wv_d), 16)

            if dbg:
                @blk.gpsimd
                def _(g):
                    W(g, ENGG, ld=ldw)
                    P.inc("gd", g.dma_start(out=dbgq_d, in_=WQS), 16)

            YSTG = SCR[:, 14384:15408]
            pk_mm = [0] * NTT
            rec_dv = [0] * (NTT + 1)
            ev_dv = [0] * NTT
            st_gd = [0] * NTT

            def emit_rec(t):
                @blk.vector
                def _(v):
                    if t == 0:
                        W(v, ENGV, ld=ldw, gp=ln_gp[0])
                    else:
                        W(v, ENGV, mm=pk_mm[t - 1])
                    yh = fap(YTH, t * 128, [[T, DC], [1, 128]])
                    yl = fap(YTL, t * 128, [[T, DC], [1, 128]])
                    yo = fap(YSTG, 0, [[128, DC], [1, 128]])
                    rec_dv[t] = P.inc("dv", v.tensor_tensor(yo, yh, yl, ALU.add))

            emit_rec(0)
            for t in range(NTT):
                pset = (t % 2) * 4

                @blk.tensor
                def _(te, t=t, pset=pset):
                    W(te, ENGT, ld=ldw, dv=rec_dv[t])
                    if t == 0:
                        W(te, ENGT, dv=P.ctr["dv"])
                    if t >= 2:
                        W(te, ENGT, dv=ev_dv[t - 2])
                    for cci in range(DC):
                        st_, sp_ = cci == 0, cci == DC - 1
                        ysl = YSTG[:, cci * 128: cci * 128 + 128]
                        P.inc("mm", te.matmul(PS[:, pset * 512:pset * 512 + 256], ysl,
                                              WQS[:, cci * 256:cci * 256 + 256], start=st_, stop=sp_))
                        P.inc("mm", te.matmul(PS[:, (pset + 1) * 512:(pset + 1) * 512 + 256], ysl,
                                              WKS[:, cci * 256:cci * 256 + 256], start=st_, stop=sp_))
                        P.inc("mm", te.matmul(PS[:, (pset + 2) * 512:(pset + 2) * 512 + 512], ysl,
                                              WVS[:, cci * 1024:cci * 1024 + 512], start=st_, stop=sp_))
                        P.inc("mm", te.matmul(PS[:, (pset + 3) * 512:(pset + 3) * 512 + 512], ysl,
                                              WVS[:, cci * 1024 + 512:cci * 1024 + 1024], start=st_, stop=sp_))
                    pk_mm[t] = P.ctr["mm"]

                if t + 1 < NTT:
                    emit_rec(t + 1)

                @blk.vector
                def _(v, t=t, pset=pset):
                    W(v, ENGV, mm=pk_mm[t])
                    if t >= 1:
                        W(v, ENGV, gd=st_gd[t - 1])
                    P.inc("dv", v.tensor_copy(QKV[:, 0:256], PS[:, pset * 512:pset * 512 + 256]))
                    P.inc("dv", v.tensor_copy(QKV[:, 256:512], PS[:, (pset + 1) * 512:(pset + 1) * 512 + 256]))
                    P.inc("dv", v.tensor_copy(QKV[:, 512:1024], PS[:, (pset + 2) * 512:(pset + 2) * 512 + 512]))
                    P.inc("dv", v.tensor_copy(QKV[:, 1024:1536], PS[:, (pset + 3) * 512:(pset + 3) * 512 + 512]))

                if dbg and t == NTT - 1:
                    evac3_dv = P.ctr["dv"]

                    @blk.gpsimd
                    def _(g):
                        W(g, ENGG, dv=evac3_dv)
                        P.inc("gd", g.dma_start(out=dbgs_d, in_=QKV), 16)

                for (seg, red_out, red_in) in (
                    (0, fap(NRM, 0, [[1, 32], [0, 1]]), fap(SQK, 0, [[16, 32], [1, 16]])),
                    (1, fap(NRM, 32, [[1, 8], [0, 1]]), fap(SQK, 0, [[64, 8], [1, 64]])),
                    (2, fap(NRM, 40, [[1, 8], [0, 1]]), fap(SQK, 0, [[64, 8], [1, 64]])),
                ):
                    @blk.scalar
                    def _(a, seg=seg):
                        W(a, ENGA, dv=P.ctr["dv"])
                        P.inc("ac", a.activation(out=SQK, in_=QKV[:, seg * 512:(seg + 1) * 512],
                                                 func=AF.Square))

                    @blk.vector
                    def _(v, red_out=red_out, red_in=red_in):
                        W(v, ENGV, ac=P.ctr["ac"])
                        P.inc("dv", v.tensor_reduce(red_out, red_in, AX.X, ALU.add))

                @blk.vector
                def _(v, t=t):
                    P.inc("dv", v.tensor_copy(SQK[:, 0:8], SQK[:, 8:16]))
                    P.inc("dv", v.tensor_single_scalar(NRM, NRM, NRM_EPS * NRM_EPS, ALU.max))

                @blk.scalar
                def _(a):
                    W(a, ENGA, dv=P.ctr["dv"])
                    P.inc("ac", a.activation(out=NRM, in_=NRM, func=AF.Sqrt))

                @blk.vector
                def _(v, t=t):
                    W(v, ENGV, ac=P.ctr["ac"])
                    P.inc("dv", v.reciprocal(NRM, NRM))
                    qv = fap(QKV, 0, [[16, 16], [1, 16]])
                    kv_ = fap(QKV, 256, [[16, 16], [1, 16]])
                    vv = fap(QKV, 512, [[64, 16], [1, 64]])
                    P.inc("dv", v.tensor_tensor(qv, qv, fap(NRM, 0, [[1, 16], [0, 16]]), ALU.mult))
                    P.inc("dv", v.tensor_tensor(kv_, kv_, fap(NRM, 16, [[1, 16], [0, 16]]), ALU.mult))
                    P.inc("dv", v.tensor_tensor(vv, vv, fap(NRM, 32, [[1, 16], [0, 64]]), ALU.mult))
                    qh0 = fap(QHL, 0, [[64, 16], [1, 16]])
                    ql1 = fap(QHL, 16, [[64, 16], [1, 16]])
                    qh2 = fap(QHL, 32, [[64, 16], [1, 16]])
                    ql3 = fap(QHL, 48, [[64, 16], [1, 16]])
                    P.inc("dv", v.tensor_copy(qh0, qv))
                    P.inc("dv", v.tensor_tensor(ql1, qv, qh0, ALU.subtract))
                    P.inc("dv", v.tensor_copy(qh2, qh0))
                    P.inc("dv", v.tensor_copy(ql3, ql1))
                    kh0 = fap(KHL, 0, [[64, 16], [1, 16]])
                    kl1 = fap(KHL, 16, [[64, 16], [1, 16]])
                    kl2 = fap(KHL, 32, [[64, 16], [1, 16]])
                    kh3 = fap(KHL, 48, [[64, 16], [1, 16]])
                    P.inc("dv", v.tensor_copy(kh0, kv_))
                    P.inc("dv", v.tensor_tensor(kl1, kv_, kh0, ALU.subtract))
                    P.inc("dv", v.tensor_copy(kl2, kl1))
                    ev_dv[t] = P.inc("dv", v.tensor_copy(kh3, kh0))

                @blk.gpsimd
                def _(g, t=t):
                    W(g, ENGG, dv=ev_dv[t])
                    P.inc("gd", g.dma_start(out=qhl_d[t * 128:(t + 1) * 128, :], in_=QHL), 16)
                    P.inc("gd", g.dma_start(out=khl_d[t * 128:(t + 1) * 128, :], in_=KHL), 16)
                    st_gd[t] = P.inc("gd", g.dma_start(out=vht_d[t * 128:(t + 1) * 128, :],
                                                       in_=QKV[:, 512:1536]), 16)
                    if t in (NTT // 2 - 1, NTT - 1):
                        c = 0 if t == NTT // 2 - 1 else 1
                        W(g, ENGG, gd=st_gd[t])
                        r0, r1 = c * (T // 2), (c + 1) * (T // 2)
                        P.inc("cc", g.collective_compute(
                            "AllGather", ALU.bypass, replica_groups=[[0, 1, 2, 3], [4, 5, 6, 7]],
                            ins=[khl_d[r0:r1, :].opt()], outs=[khl_all[c].opt()]))
                        cc_chunk[c] = P.inc("cc", g.collective_compute(
                            "AllGather", ALU.bypass, replica_groups=[[0, 1, 2, 3], [4, 5, 6, 7]],
                            ins=[vht_d[r0:r1, :].opt()], outs=[vht_all[c].opt()]))
            return [st_gd[NTT // 2 - 1], st_gd[NTT - 1]]

        outsrc = XT
        ln_gp = [0]
        if stage == "ln1":
            ln_gp[0] = emit_ln()
            outsrc = XT
        elif stage == "gate1":
            ln_gp[0] = emit_ln()
            W8a = emit_gating()

            @blk.gpsimd
            def _(g):
                W(g, ENGG, dv=P.ctr["dv"], gp=ln_gp[0])
                P.inc("gd", g.dma_start(out=dbgw_d, in_=SCR[0:8, 9728:9728 + T]), 16)
                P.inc("gd", g.dma_start(out=dbgl_d, in_=LOG), 16)
                P.inc("gd", g.dma_start(out=dbgf_d, in_=WFT), 16)
                P.inc("gd", g.dma_start(out=dap(khl_dbg, 0, [[16384, 128], [1, 16384]]),
                                        in_=YTH), 16)
                P.inc("gd", g.dma_start(out=dap(vht_dbg, 0, [[16384, 128], [1, 16384]]),
                                        in_=YTL), 16)
            outsrc = XT
        elif stage != "xio":
            ln_gp[0] = emit_ln()
            W8a = emit_gating()
            emit_ffn(W8a)
            if dbg and stage == "ffn1":
                @blk.gpsimd
                def _(g):
                    W(g, ENGG, dv=P.ctr["dv"])
                    P.inc("gd", g.dma_start(out=dbgw_d, in_=SCR[0:8, 9728:9728 + T]), 16)
                    P.inc("gd", g.dma_start(out=dbgl_d, in_=LOG), 16)
                    P.inc("gd", g.dma_start(out=dbgf_d, in_=WFT), 16)

        qkv_gd_half = [0, 0]
        cc_chunk = [0, 0]
        if stage in ("qkv", "attn", "full"):
            ln_gp[0] = emit_ln()
            qkv_gd_half = emit_qkv()

        @blk.gpsimd
        def _(g):
            if stage not in ("qkv", "attn", "full"):
                return
            if dbg:
                W(g, ENGG, cc=cc_chunk[1])
                for c in range(2):
                    P.inc("gd", g.dma_start(out=khl_dbg[c * (NK // 2):(c + 1) * (NK // 2), :],
                                            in_=khl_all[c]), 16)
                    P.inc("gd", g.dma_start(out=vht_dbg[c * (NK // 2):(c + 1) * (NK // 2), :],
                                            in_=vht_all[c]), 16)
                P.inc("gd", g.dma_start(out=qhl_dbg, in_=qhl_d), 16)

        VSL = [SCR[:, 0:1024], SCR[:, 1024:2048]]
        A2B = [SCR[:, 4096:5120], SCR[:, 5120:6144], SCR[:, 7168:8192]]
        R2B = [SCR[:, 2048:3072], SCR[:, 3072:4096], SCR[:, 6144:7168]]
        OSS = SCR[:, 8192:10240]
        SQ2 = SCR[:, 10240:10752]
        NR2 = SCR[0:2, 10752:10752 + T]
        THR = SCR[0:2, 4096:4096 + T]
        QST = [BSC[:, 0:2048], BSC[:, 2048:4096]]
        KST = [BSC[:, 4096:5120], BSC[:, 5120:6144]]

        qst_ld = [0] * HPAIR
        kst_ld = {}
        vsl_ld = {}
        s_mm = {}
        relu_ac = {}
        mult_dv = {}
        av_mm = {}
        av_mmj = {}
        pair_done_dv = [0] * HPAIR

        attn_start_dv = P.ctr["dv"]
        run_attn = stage in ("attn", "full")
        jmeta = {}

        def emit_s(p, i, sl, kc, qt, j):
            jmeta[(p, j)] = (i, sl, kc, qt)
            sb = j % 2

            @blk.tensor
            def _(te):
                W(te, ENGT, ld=max(kst_ld[(p, sl)], qst_ld[p]))
                if j >= 2:
                    W(te, ENGT, dv=mult_dv[(p, j - 2)])
                elif p >= 1:
                    W(te, ENGT, dv=max(mult_dv[(p - 1, NJ - 2 + j)],
                                       pair_done_dv[p - 1]))
                else:
                    W(te, ENGT, dv=attn_start_dv)
                b0 = (4 + sb * 2) * 512
                P.inc("mm", te.matmul(PS[:, b0:b0 + 512],
                                      KST[i % 2][0:64, kc * 128:kc * 128 + 128],
                                      QST[p % 2][0:64, qt * 512:qt * 512 + 512],
                                      start=True, stop=True))
                s_mm[(p, sl, kc, qt)] = P.inc("mm", te.matmul(
                    PS[:, b0 + 512:b0 + 1024],
                    KST[i % 2][64:128, kc * 128:kc * 128 + 128],
                    QST[p % 2][64:128, qt * 512:qt * 512 + 512],
                    start=True, stop=True))

        def emit_evac(p, j):
            i, sl, kc, qt = jmeta[(p, j)]
            sb2 = j % 2
            sb3 = j % 3
            b0 = (4 + sb2 * 2) * 512
            on_gp = (j % 3 != 2)

            @blk.vector
            def _(v):
                W(v, ENGV, mm=s_mm[(p, sl, kc, qt)])
                jj = j - 3
                pp = p
                if jj < 0 and p >= 1:
                    pp, jj = p - 1, NJ + jj
                if jj >= 0:
                    pk = ("gp" if jj % 3 != 2 else "ac")
                    W(v, ENGV, **{pk: relu_ac[(pp, jj)]})
                mult_dv[(p, j)] = P.inc("dv", v.tensor_scalar(
                    R2B[sb3], PS[:, b0:b0 + 1024], 0.0, None, ALU.max))

            def sq_waits(eng, ename):
                W(eng, ename, dv=mult_dv[(p, j)])
                jj = j - 3
                pp = p
                if jj < 0 and p >= 1:
                    pp, jj = p - 1, NJ + jj
                    W(eng, ename, dv=pair_done_dv[p - 1])
                if jj >= 0:
                    W(eng, ename, mm=av_mmj[(pp, jj)])

            if on_gp:
                @blk.gpsimd
                def _(g):
                    sq_waits(g, ENGG)
                    relu_ac[(p, j)] = P.inc("gp", g.tensor_tensor(
                        A2B[sb3], R2B[sb3], R2B[sb3], ALU.mult))
            else:
                @blk.scalar
                def _(a):
                    sq_waits(a, ENGA)
                    relu_ac[(p, j)] = P.inc("ac", a.activation(
                        out=A2B[sb3], in_=R2B[sb3], func=AF.Square))

        def emit_av(p, j):
            i, sl, kc, qt = jmeta[(p, j)]
            sb3 = j % 3
            sq_sem = "gp" if (j % 3 != 2) else "ac"

            @blk.tensor
            def _(te):
                W(te, ENGT, ld=vsl_ld[(p, sl)], **{sq_sem: relu_ac[(p, j)]})
                st_ = (j < NQT)
                sp_ = (j >= NJ - NQT)
                vb = kc * 128
                P.inc("mm", te.matmul(PS[0:64, qt * 512:qt * 512 + 512],
                                      VSL[i % 2][:, vb:vb + 64], A2B[sb3][:, 0:512],
                                      start=st_, stop=sp_, tile_position=(0, 0)))
                av_mm[(p, sl, kc, qt)] = P.inc("mm", te.matmul(
                    PS[64:128, qt * 512:qt * 512 + 512],
                    VSL[i % 2][:, vb + 64:vb + 128], A2B[sb3][:, 512:1024],
                    start=st_, stop=sp_, tile_position=(0, 64)))
                av_mmj[(p, j)] = av_mm[(p, sl, kc, qt)]

        @blk.sync
        def _(y):
            if not run_attn:
                return
            W(y, ENGY, gd=P.ctr["gd"], mm=P.ctr["mm"], dv=P.ctr["dv"])
            qst_ld[0] = P.inc("ld", y.dma_start(out=QST[0], in_=qhl_d[:, 0:128], transpose=True), 16)

        for p in (range(HPAIR) if run_attn else ()):
            @blk.sync
            def _(y, p=p):
                W(y, ENGY, cc=cc_chunk[0])
                if p + 1 < HPAIR:
                    if p >= 1:
                        W(y, ENGY, mm=s_mm[(p - 1, NSLAB - 1, NKC - 1, NQT - 1)])
                    qst_ld[p + 1] = P.inc("ld", y.dma_start(
                        out=QST[(p + 1) % 2], in_=qhl_d[:, (p + 1) * 128:(p + 2) * 128],
                        transpose=True), 16)

            for sidx, sl in enumerate((0, 2, 4, 6, 1, 3, 5, 7)):
                i = p * NSLAB + sidx

                @blk.sync
                def _(y, p=p, sl=sl, i=i):
                    if i >= 2:
                        order = (0, 2, 4, 6, 1, 3, 5, 7)
                        pi, si = (i - 2) // NSLAB, order[(i - 2) % NSLAB]
                        W(y, ENGY, mm=av_mm[(pi, si, NKC - 1, NQT - 1)])
                    ch, rk = sl % 2, sl // 2
                    W(y, ENGY, cc=cc_chunk[ch])
                    kst_ld[(p, sl)] = P.inc("ld", y.dma_start(
                        out=KST[i % 2],
                        in_=khl_all[ch][rk * 1024:(rk + 1) * 1024, p * 128:(p + 1) * 128],
                        transpose=True), 16)
                    src = dap(vht_all[ch], (rk * 1024) * D + p * 128, [[D, 128], [128 * D, 8], [1, 128]])
                    vsl_ld[(p, sl)] = P.inc("ld", y.dma_start(
                        out=fap(VSL[i % 2], 0, [[128, 8], [1, 128]]), in_=src), 16)

                for kc in range(NKC):
                    for qt in range(NQT):
                        j = (sidx * NKC + kc) * NQT + qt
                        emit_s(p, i, sl, kc, qt, j)
                        if j >= 1:
                            emit_evac(p, j - 1)
                        if j >= 2:
                            emit_av(p, j - 2)
                if sidx == NSLAB - 1:
                    emit_evac(p, NJ - 1)
                    emit_av(p, NJ - 2)
                    emit_av(p, NJ - 1)

            oss_dv = [0] * NQT
            sq_ac = [0] * NQT
            n2_mm = [0] * NQT
            for qt in range(NQT):
                @blk.vector
                def _(v, p=p, qt=qt):
                    W(v, ENGV, mm=av_mm[(p, NSLAB - 1, NKC - 1, qt)])
                    oss_dv[qt] = P.inc("dv", v.tensor_copy(OSS[:, qt * 512:qt * 512 + 512],
                                                           PS[:, qt * 512:qt * 512 + 512]))

                @blk.scalar
                def _(a, qt=qt):
                    W(a, ENGA, dv=oss_dv[qt])
                    if qt >= 1:
                        W(a, ENGA, mm=n2_mm[qt - 1])
                    sq_ac[qt] = P.inc("ac", a.activation(
                        out=SQ2, in_=OSS[:, qt * 512:qt * 512 + 512], func=AF.Square))

                @blk.tensor
                def _(te, p=p, qt=qt):
                    W(te, ENGT, ac=sq_ac[qt])
                    n2_mm[qt] = P.inc("mm", te.matmul(
                        PS[0:2, (4 + qt) * 512:(4 + qt) * 512 + 512], BLK2, SQ2,
                        start=True, stop=True))

                @blk.vector
                def _(v, qt=qt):
                    W(v, ENGV, mm=n2_mm[qt])
                    P.inc("dv", v.tensor_copy(NR2[0:2, qt * 512:qt * 512 + 512],
                                              PS[0:2, (4 + qt) * 512:(4 + qt) * 512 + 512]))

            @blk.scalar
            def _(a):
                W(a, ENGA, dv=P.ctr["dv"])
                P.inc("ac", a.activation(out=NR2, in_=NR2, func=AF.Sqrt))

            @blk.vector
            def _(v):
                W(v, ENGV, ac=P.ctr["ac"])
                P.inc("dv", v.tensor_single_scalar(THR, NR2, NRM_EPS, ALU.max))
                P.inc("dv", v.reciprocal(THR, THR))

            @blk.scalar
            def _(a):
                W(a, ENGA, dv=P.ctr["dv"])
                P.inc("ac", a.activation(out=NR2, in_=NR2, func=AF.Tanh))

            @blk.vector
            def _(v):
                W(v, ENGV, ac=P.ctr["ac"])
                P.inc("dv", v.tensor_tensor(NR2, NR2, THR, ALU.mult))

            for qt in range(NQT):
                bq_mm = 0

                @blk.tensor
                def _(te, qt=qt):
                    W(te, ENGT, dv=P.ctr["dv"])
                    nonlocal bq_mm
                    bq_mm = P.inc("mm", te.matmul(
                        PS[:, (4 + qt) * 512:(4 + qt) * 512 + 512], SEL2,
                        NR2[0:2, qt * 512:qt * 512 + 512], start=True, stop=True))

                @blk.vector
                def _(v, p=p, qt=qt):
                    W(v, ENGV, mm=bq_mm)
                    P.inc("dv", v.tensor_tensor(
                        SQ2, OSS[:, qt * 512:qt * 512 + 512],
                        PS[:, (4 + qt) * 512:(4 + qt) * 512 + 512], ALU.mult))
                    ath = YTH[:, p * T + qt * 512: p * T + qt * 512 + 512]
                    atl = YTL[:, p * T + qt * 512: p * T + qt * 512 + 512]
                    P.inc("dv", v.tensor_copy(ath, SQ2))
                    pair_done_dv[p] = P.inc("dv", v.tensor_tensor(atl, SQ2, ath, ALU.subtract))

        if dbg and stage == "attn":
            @blk.gpsimd
            def _(g):
                W(g, ENGG, dv=pair_done_dv[HPAIR - 1])
                P.inc("gd", g.dma_start(out=att_dbg, in_=YTH), 16)

        WOS = SCR[:, 0:8192]
        wo_ld = 0

        @blk.sync
        def _(y):
            nonlocal wo_ld
            if not run_attn:
                return
            W(y, ENGY, dv=P.ctr["dv"], mm=P.ctr["mm"], ac=P.ctr["ac"])
            wo_ld = P.inc("ld", y.dma_start(out=WOS, in_=wo_d), 16)

        wo_add_dv = {}
        Y32 = SCR[:, 8192:12288]
        rec32_dv = [0] * NMT
        for mt in (range(NMT) if run_attn else ()):
            @blk.vector
            def _(v, mt=mt):
                if mt == 0:
                    W(v, ENGV, ld=wo_ld)
                    W(v, ENGV, dv=pair_done_dv[HPAIR - 1])
                if mt >= 1:
                    W(v, ENGV, mm=P.ctr["mm"])
                for cci in range(DC):
                    yh = YTH[:, cci * T + mt * 512: cci * T + mt * 512 + 512]
                    yl = YTL[:, cci * T + mt * 512: cci * T + mt * 512 + 512]
                    P.inc("dv", v.tensor_tensor(Y32[:, cci * 512:cci * 512 + 512], yh, yl, ALU.add))
                rec32_dv[mt] = P.ctr["dv"]

            for dd in range(DC):
                k = mt * DC + dd
                wo_mm = 0

                @blk.tensor
                def _(te, dd=dd, mt=mt, k=k):
                    W(te, ENGT, ld=wo_ld, dv=rec32_dv[mt])
                    if k >= 2:
                        W(te, ENGT, dv=wo_add_dv[k - 2])
                    for cci in range(DC):
                        P.inc("mm", te.matmul(PS[:, (k % 2) * 512:(k % 2) * 512 + 512],
                                              WOS[:, cci * 1024 + dd * 128: cci * 1024 + dd * 128 + 128],
                                              Y32[:, cci * 512:cci * 512 + 512],
                                              start=(cci == 0), stop=(cci == DC - 1)))
                    nonlocal wo_mm
                    wo_mm = P.ctr["mm"]

                @blk.vector
                def _(v, dd=dd, mt=mt, k=k):
                    W(v, ENGV, mm=wo_mm)
                    xs = XT[:, dd * T + mt * 512: dd * T + mt * 512 + 512]
                    wo_add_dv[k] = P.inc("dv", v.tensor_tensor(
                        xs, xs, PS[:, (k % 2) * 512:(k % 2) * 512 + 512], ALU.add))

        if stage == "full":
            ln_gp[0] = emit_ln()
            W8b = emit_gating()
            emit_ffn(W8b)

        ot_dv = [0] * NTT
        ot_gd = [0] * NTT
        for t in range(NTT):
            ot_mm = 0

            @blk.tensor
            def _(te, t=t):
                W(te, ENGT, dv=last_xt_dv.get((NE * DC - 1, t // NMT), P.ctr["dv"]))
                if t >= 2:
                    W(te, ENGT, dv=ot_dv[t - 2])
                g = (t % 2) * 2 + 4
                for c in range(DC):
                    P.inc("mm", te.transpose(PS[:, g * 512 + c * 128: g * 512 + c * 128 + 128],
                                             outsrc[:, c * T + t * 128: c * T + t * 128 + 128], IDN))
                nonlocal ot_mm
                ot_mm = P.ctr["mm"]

            @blk.vector
            def _(v, t=t):
                W(v, ENGV, mm=ot_mm)
                if t >= 2:
                    W(v, ENGV, gd=ot_gd[t - 2])
                g = (t % 2) * 2 + 4
                src = fap(PS, g * 512, [[128, 8], [1, 128]])
                ot_dv[t] = P.inc("dv", v.tensor_copy(SCR[:, (t % 2) * 1024:(t % 2) * 1024 + 1024], src))

            @blk.gpsimd
            def _(g_, t=t):
                W(g_, ENGG, dv=ot_dv[t])
                ot_gd[t] = P.inc("gd", g_.dma_start(out=out_d[t * 128:(t + 1) * 128, :],
                                                    in_=SCR[:, (t % 2) * 1024:(t % 2) * 1024 + 1024]), 16)

        @blk.gpsimd
        def _(g_):
            W(g_, ENGG, gd=P.ctr["gd"])

    return nc


_PROG = None


def kernel(L0_attn_tokens, L0_attn_Wq, L0_attn_Wkv, L0_attn_Wout, L0_ffn_gate_w, L0_ffn_expert_w):
    global _PROG
    x = np.ascontiguousarray(np.asarray(L0_attn_tokens, dtype=np.float32))

    def sbimg(w):
        n = w.shape[1]
        return np.ascontiguousarray(w.reshape(8, 128, n).transpose(1, 0, 2).reshape(128, 8 * n))

    wq = sbimg(np.asarray(L0_attn_Wq, dtype=np.float32))
    wkv = np.asarray(L0_attn_Wkv, dtype=np.float32)
    cols = np.arange(1280).reshape(16, 80)
    wk = sbimg(np.ascontiguousarray(wkv[:, cols[:, :16].reshape(-1)]))
    wv = sbimg(np.ascontiguousarray(wkv[:, cols[:, 16:].reshape(-1)]))
    wo = sbimg(np.ascontiguousarray(np.asarray(L0_attn_Wout, dtype=np.float32)))
    gt = sbimg(np.ascontiguousarray(np.asarray(L0_ffn_gate_w, dtype=np.float32).T))
    ew_t = np.asarray(L0_ffn_expert_w, dtype=np.float32).transpose(0, 2, 1)
    ew = np.ascontiguousarray(
        ew_t.reshape(8, 8, 128, 8, 128).transpose(0, 3, 2, 1, 4).reshape(8, 8, 128, 1024))

    if _PROG is None:
        _PROG = build()
    nc = _PROG

    in_maps = []
    for c in range(8):
        b, tc = c // 4, c % 4
        xs = np.ascontiguousarray(x[b, tc * T:(tc + 1) * T, :])
        in_maps.append({"x": xs, "wq": wq, "wk": wk, "wv": wv, "wo": wo, "gt": gt, "ew": ew})
    res = run_bass_kernel_spmd(nc, in_maps, core_ids=list(range(8)))
    out = np.empty((2, 8192, 1024), np.float32)
    for c in range(8):
        b, tc = c // 4, c % 4
        out[b, tc * T:(tc + 1) * T, :] = res.results[c]["out"]
    return out


# version 15 (final)
# speedup vs baseline: 1.1020x; 1.1020x over previous
import numpy as np
import concourse.bass as bass
import concourse.mybir as mybir
from concourse.bass_utils import run_bass_kernel_spmd

F32 = mybir.dt.float32
BF16 = mybir.dt.bfloat16
AF = mybir.ActivationFunctionType
ALU = mybir.AluOpType
AX = mybir.AxisListType

T = 2048
D = 1024
DC = 8
NTT = 16
NMT = 4
H = 16
HPAIR = 8
NK = 8192
NSLAB = 8
NKC = 8
NE = 8
NQT = 4
NJ = NSLAB * NKC * NQT
LN_EPS = 1e-5
NRM_EPS = 1e-12

ENGY, ENGT, ENGV, ENGA, ENGG = "sync", "tensor", "vector", "scalar", "gpsimd"


class Prog:
    def __init__(self, nc, stack):
        self.nc = nc
        self.sem = {}
        self.ctr = {}
        for n in ("mm", "dv", "ac", "gp", "ld", "gd", "cc"):
            self.sem[n] = stack.enter_context(nc.semaphore(n))
            self.ctr[n] = 0
        self.lastw = {}

    def inc(self, name, instr, n=1):
        instr.then_inc(self.sem[name], n)
        self.ctr[name] += n
        return self.ctr[name]

    def wait(self, eng, engname, sname, val):
        if val is None or val <= 0:
            return
        key = (engname, sname)
        if self.lastw.get(key, -1) >= val:
            return
        eng.wait_ge(self.sem[sname], val)
        self.lastw[key] = val


def fap(base_ap, col_off, dims):
    return bass.AP(base_ap.tensor, base_ap.offset + col_off,
                   [list(base_ap.ap[0])] + [list(d) for d in dims])


def dap(dram_ap, el_off, dims):
    return bass.AP(dram_ap.tensor, dram_ap.offset + el_off, [list(d) for d in dims])


def build(stage="full", dbg=False):
    import contextlib
    nc = bass.Bass("TRN2", target_bir_lowering=False)

    x_d = nc.dram_tensor("x", [T, D], F32, kind="ExternalInput").ap()
    wq_d = nc.dram_tensor("wq", [128, 2048], F32, kind="ExternalInput").ap()
    wk_d = nc.dram_tensor("wk", [128, 2048], F32, kind="ExternalInput").ap()
    wv_d = nc.dram_tensor("wv", [128, 8192], F32, kind="ExternalInput").ap()
    wo_d = nc.dram_tensor("wo", [128, 8192], F32, kind="ExternalInput").ap()
    gt_d = nc.dram_tensor("gt", [128, 64], F32, kind="ExternalInput").ap()
    ew_d = nc.dram_tensor("ew", [NE, DC, 128, 1024], F32, kind="ExternalInput").ap()
    out_d = nc.dram_tensor("out", [T, D], F32, kind="ExternalOutput").ap()

    qhl_d = nc.dram_tensor("qhl_d", [T, H * 64], BF16).ap()
    khl_d = nc.dram_tensor("khl_d", [T, H * 64], BF16).ap()
    vht_d = nc.dram_tensor("vht_d", [T, D], F32).ap()
    dbgw_d = (nc.dram_tensor("dbgw", [8, T], F32, kind="ExternalOutput").ap()
              if dbg else None)
    dbgl_d = (nc.dram_tensor("dbgl", [128, NTT * 8], F32, kind="ExternalOutput").ap()
              if dbg else None)
    dbgf_d = (nc.dram_tensor("dbgf", [128, NTT * 8], F32, kind="ExternalOutput").ap()
              if dbg else None)
    khl_all = [nc.dram_tensor(f"khl_all{c}", [NK // 2, H * 64], BF16).ap() for c in range(2)]
    vht_all = [nc.dram_tensor(f"vht_all{c}", [NK // 2, D], F32).ap() for c in range(2)]
    khl_dbg = (nc.dram_tensor("khl_dbg", [NK, H * 64], BF16, kind="ExternalOutput").ap()
               if dbg else None)
    att_dbg = (nc.dram_tensor("att_dbg", [128, DC * T], F32, kind="ExternalOutput").ap()
               if dbg else None)
    qhl_dbg = (nc.dram_tensor("qhl_dbg", [T, H * 64], BF16, kind="ExternalOutput").ap()
               if dbg else None)
    dbgq_d = (nc.dram_tensor("dbgq", [128, 2048], F32, kind="ExternalOutput").ap()
              if dbg else None)
    dbgs_d = (nc.dram_tensor("dbgs", [128, 1536], F32, kind="ExternalOutput").ap()
              if dbg else None)
    vht_dbg = (nc.dram_tensor("vht_dbg", [NK, D], F32, kind="ExternalOutput").ap()
               if dbg else None)

    idn_c = nc.inline_tensor(np.eye(128, dtype=np.float32), "idn_c")
    ones1_c = nc.inline_tensor(np.ones((1, 128), dtype=np.float32), "ones1_c")
    onesc_c = nc.inline_tensor(np.ones((128, 1), dtype=np.float32), "onesc_c")
    sel2_np = np.zeros((2, 128), dtype=np.float32)
    sel2_np[0, 0:64] = 1.0
    sel2_np[1, 64:128] = 1.0
    sel2_c = nc.inline_tensor(sel2_np, "sel2_c")
    blk2_np = np.zeros((128, 2), dtype=np.float32)
    blk2_np[0:64, 0] = 1.0
    blk2_np[64:128, 1] = 1.0
    blk2_c = nc.inline_tensor(blk2_np, "blk2_c")
    sels_np = np.zeros((8, 8 * 128), dtype=np.float32)
    for e in range(8):
        sels_np[e, e * 128:(e + 1) * 128] = 1.0
    sels_c = nc.inline_tensor(sels_np, "sels_c")
    tie_np = np.zeros((128, 64), dtype=np.float32)
    for e in range(8):
        for e2 in range(e):
            tie_np[:, e * 8 + e2] = 1.0
    tie_c = nc.inline_tensor(tie_np, "tie_c")

    XT = nc.alloc_sbuf_tensor("XT", [128, DC * T], F32).ap()
    YTH = nc.alloc_sbuf_tensor("YTH", [128, DC * T], BF16).ap()
    YTL = nc.alloc_sbuf_tensor("YTL", [128, DC * T], BF16).ap()
    IDN = nc.alloc_sbuf_tensor("IDN", [128, 128], F32).ap()
    ONES1 = nc.alloc_sbuf_tensor("ONES1", [1, 128], F32).ap()
    ONESC = nc.alloc_sbuf_tensor("ONESC", [128, 1], F32).ap()
    SEL2 = nc.alloc_sbuf_tensor("SEL2", [2, 128], F32).ap()
    BLK2 = nc.alloc_sbuf_tensor("BLK2", [128, 2], F32).ap()
    SELS = nc.alloc_sbuf_tensor("SELS", [8, 8 * 128], F32).ap()
    TIE = nc.alloc_sbuf_tensor("TIE", [128, 64], F32).ap()
    LOG = nc.alloc_sbuf_tensor("LOG", [128, NTT * 8], F32).ap()
    WFT = nc.alloc_sbuf_tensor("WFT", [128, NTT * 8], F32).ap()
    SCR = nc.alloc_sbuf_tensor("SCR", [128, 15424], F32).ap()
    BSC = nc.alloc_sbuf_tensor("BSC", [128, 6144], BF16).ap()
    PS = nc.alloc_psum_tensor("PS", [128, 4096], F32).ap()

    stack = contextlib.ExitStack()
    with stack:
        P = Prog(nc, stack)
        blk = stack.enter_context(nc.Block())

        def W(eng, ename, **kw):
            for sname, val in kw.items():
                P.wait(eng, ename, sname, val)

        ld_x = [0] * NTT
        tin_v = [0] * NTT

        @blk.sync
        def _(y):
            for cst, dst in ((idn_c, IDN), (ones1_c, ONES1), (onesc_c, ONESC), (sel2_c, SEL2),
                             (blk2_c, BLK2), (sels_c, SELS), (tie_c, TIE)):
                P.inc("ld", y.dma_start(out=dst, in_=cst.ap()), 16)

        for t in range(NTT):
            xst = SCR[:, (t % 2) * 1024:(t % 2) * 1024 + 1024]

            @blk.sync
            def _(y, t=t, xst=xst):
                if t >= 2:
                    W(y, ENGY, dv=tin_v[t - 2])
                ld_x[t] = P.inc("ld", y.dma_start(out=xst, in_=x_d[t * 128:(t + 1) * 128, :]), 16)

            @blk.tensor
            def _(te, t=t, xst=xst):
                W(te, ENGT, ld=ld_x[t])
                if t >= 2:
                    W(te, ENGT, dv=tin_v[t - 2])
                g = (t % 2) * 2
                for c in range(DC):
                    P.inc("mm", te.transpose(PS[:, g * 512 + c * 128: g * 512 + c * 128 + 128],
                                             xst[:, c * 128:c * 128 + 128], IDN))

            @blk.vector
            def _(v, t=t):
                W(v, ENGV, mm=P.ctr["mm"])
                g = (t % 2) * 2
                src = fap(PS, g * 512, [[128, 8], [1, 128]])
                dst = fap(XT, t * 128, [[T, 8], [1, 128]])
                tin_v[t] = P.inc("dv", v.tensor_copy(dst, src))


        def emit_ln():
            SQ = [SCR[:, 0:512], SCR[:, 512:1024]]
            MT = SCR[:, 3072:5120]
            RT = SCR[:, 5120:7168]
            s_r = SCR[0:1, 1024:1024 + T]
            ss_r = SCR[0:1, 3072:3072 + T]
            t_r = SCR[0:1, 5120:5120 + T]
            r_r = t_r
            mr_r = s_r
            start_dv = P.ctr["dv"]
            start_ac = P.ctr["ac"]
            start_mm = P.ctr["mm"]
            sq_a = [0] * (DC * NMT)
            last_ss_mm = [0, 0]

            @blk.tensor
            def _(te):
                W(te, ENGT, dv=start_dv, ac=start_ac)
                for cci in range(DC):
                    for nq in range(NMT):
                        P.inc("mm", te.matmul(PS[0:1, nq * 512:nq * 512 + 512], ONESC,
                                              XT[:, cci * T + nq * 512: cci * T + nq * 512 + 512],
                                              start=(cci == 0), stop=(cci == DC - 1)))

            for idx in range(DC * NMT):
                cci, nq = idx // NMT, idx % NMT

                @blk.scalar
                def _(a, idx=idx, cci=cci, nq=nq):
                    if idx >= 2:
                        W(a, ENGA, mm=last_ss_mm[idx % 2])
                    else:
                        W(a, ENGA, dv=start_dv, mm=start_mm)
                    sq_a[idx] = P.inc("ac", a.activation(
                        out=SQ[idx % 2], in_=XT[:, cci * T + nq * 512: cci * T + nq * 512 + 512],
                        func=AF.Square))

                @blk.tensor
                def _(te, idx=idx, cci=cci, nq=nq):
                    W(te, ENGT, ac=sq_a[idx])
                    last_ss_mm[idx % 2] = P.inc("mm", te.matmul(
                        PS[0:1, 2048 + nq * 512: 2048 + nq * 512 + 512], ONESC, SQ[idx % 2],
                        start=(cci == 0), stop=(cci == DC - 1)))

            @blk.vector
            def _(v):
                W(v, ENGV, mm=P.ctr["mm"])
                P.inc("dv", v.tensor_copy(s_r, PS[0:1, 0:T]))
                P.inc("dv", v.tensor_copy(ss_r, PS[0:1, 2048:2048 + T]))
                P.inc("dv", v.tensor_scalar(s_r, s_r, 1.0 / D, None, ALU.mult))
                P.inc("dv", v.tensor_scalar(ss_r, ss_r, 1.0 / D, None, ALU.mult))
                P.inc("dv", v.tensor_tensor(t_r, s_r, s_r, ALU.mult))
                P.inc("dv", v.tensor_tensor(ss_r, ss_r, t_r, ALU.subtract))
                P.inc("dv", v.tensor_scalar(ss_r, ss_r, LN_EPS, None, ALU.add))

            @blk.scalar
            def _(a):
                W(a, ENGA, dv=P.ctr["dv"])
                P.inc("ac", a.activation(out=t_r, in_=ss_r, func=AF.Sqrt))

            @blk.vector
            def _(v):
                W(v, ENGV, ac=P.ctr["ac"])
                P.inc("dv", v.reciprocal(r_r, t_r))
                P.inc("dv", v.tensor_tensor(mr_r, s_r, r_r, ALU.mult))

            bc_cp = [0] * (2 * NMT)
            for bi, (row, dst) in enumerate(((r_r, RT), (mr_r, MT))):
                for nq in range(NMT):
                    k = bi * NMT + nq
                    bc_mm = 0

                    @blk.tensor
                    def _(te, row=row, nq=nq, k=k):
                        W(te, ENGT, dv=P.ctr["dv"] if k == 0 else (bc_cp[k - 2] if k >= 2 else 0))
                        nonlocal bc_mm
                        bc_mm = P.inc("mm", te.matmul(
                            PS[:, (k % 2) * 512:(k % 2) * 512 + 512], ONES1,
                            row[0:1, nq * 512:nq * 512 + 512], start=True, stop=True))

                    @blk.vector
                    def _(v, dst=dst, nq=nq, k=k):
                        W(v, ENGV, mm=bc_mm)
                        bc_cp[k] = P.inc("dv", v.tensor_copy(
                            dst[:, nq * 512:nq * 512 + 512],
                            PS[:, (k % 2) * 512:(k % 2) * 512 + 512]))

            TMPY = SCR[:, 7232:7232 + T]
            Y2 = SCR[:, 9280:9280 + T]
            TMPYg = SCR[:, 12800:12800 + 1024]
            Y2g = SCR[:, 13824:13824 + 1024]
            ln_done_dv = 0
            ln_done_gp = 0

            GP_SPLIT = False

            @blk.vector
            def _(v):
                nonlocal ln_done_dv
                for cci in range(DC // 2 if GP_SPLIT else DC):
                    P.inc("dv", v.tensor_tensor(TMPY, XT[:, cci * T:(cci + 1) * T], RT, ALU.mult))
                    P.inc("dv", v.tensor_tensor(Y2, TMPY, MT, ALU.subtract))
                    P.inc("dv", v.tensor_copy(YTH[:, cci * T:(cci + 1) * T], Y2))
                    ln_done_dv = P.inc("dv", v.tensor_tensor(
                        YTL[:, cci * T:(cci + 1) * T], Y2,
                        YTH[:, cci * T:(cci + 1) * T], ALU.subtract))

            @blk.gpsimd
            def _(g):
                nonlocal ln_done_gp
                if not GP_SPLIT:
                    return
                W(g, ENGG, dv=P.ctr["dv"] - 4 * (DC // 2), mm=P.ctr["mm"])
                for cci in range(DC // 2, DC):
                    for half in range(2):
                        sl0 = cci * T + half * 1024
                        P.inc("gp", g.tensor_tensor(TMPYg, XT[:, sl0:sl0 + 1024],
                                                    RT[:, half * 1024:half * 1024 + 1024], ALU.mult))
                        P.inc("gp", g.tensor_tensor(Y2g, TMPYg,
                                                    MT[:, half * 1024:half * 1024 + 1024], ALU.subtract))
                        P.inc("gp", g.tensor_copy(YTH[:, sl0:sl0 + 1024], Y2g))
                        ln_done_gp = P.inc("gp", g.tensor_tensor(
                            YTL[:, sl0:sl0 + 1024], Y2g,
                            YTH[:, sl0:sl0 + 1024], ALU.subtract))
            return ln_done_gp

        def emit_gating():
            GTS = SCR[:, 7168:7232]
            GTB = BSC[:, 6016:6144]
            GS = SCR[:, 7232:8256]
            GS2 = SCR[:, 8256:9280]
            CNT = SCR[:, 9280:9408]
            SELB = SCR[:, 9408:9536]
            ELB = SCR[:, 9552:9680]
            NVM = SCR[:, 9536:9552]
            SM = SCR[:, 9680:9696]
            RSM = SCR[:, 9696:9712]
            W8 = SCR[0:8, 9728:9728 + T]
            gts_ld = 0

            @blk.sync
            def _(y):
                nonlocal gts_ld
                W(y, ENGY, dv=P.ctr["dv"], mm=P.ctr["mm"], ac=P.ctr["ac"])
                gts_ld = P.inc("ld", y.dma_start(out=GTS, in_=gt_d), 16)

            gts_dv = 0

            @blk.vector
            def _(v):
                nonlocal gts_dv
                W(v, ENGV, ld=gts_ld)
                P.inc("dv", v.tensor_copy(GTB[:, 0:64], GTS))
                gts_dv = P.inc("dv", v.tensor_tensor(GTB[:, 64:128], GTS, GTB[:, 0:64],
                                                     ALU.subtract))

            lg_mm = [0] * NTT
            cp_dv = [0] * NTT
            for t in range(NTT):
                @blk.tensor
                def _(te, t=t):
                    W(te, ENGT, dv=gts_dv)
                    if t == 0:
                        W(te, ENGT, dv=P.ctr["dv"], gp=ln_gp[0])
                    if t >= 2:
                        W(te, ENGT, dv=cp_dv[t - 2])
                    for cci in range(DC):
                        for (sta, mov) in ((YTH, GTB[:, 0:64]), (YTH, GTB[:, 64:128]),
                                           (YTL, GTB[:, 0:64])):
                            P.inc("mm", te.matmul(
                                PS[:, (t % 2) * 512:(t % 2) * 512 + 8],
                                sta[:, cci * T + t * 128: cci * T + t * 128 + 128],
                                mov[:, cci * 8:cci * 8 + 8],
                                start=(cci == 0 and sta is YTH and mov.offset == GTB.offset),
                                stop=(cci == DC - 1 and sta is YTL)))
                    lg_mm[t] = P.ctr["mm"]

                @blk.vector
                def _(v, t=t):
                    W(v, ENGV, mm=lg_mm[t])
                    cp_dv[t] = P.inc("dv", v.tensor_copy(LOG[:, t * 8:t * 8 + 8],
                                                         PS[:, (t % 2) * 512:(t % 2) * 512 + 8]))

            @blk.vector
            def _(v):
                l_ee = fap(LOG, 0, [[8, NTT], [0, 8], [1, 8]])
                l_e = fap(LOG, 0, [[8, NTT], [1, 8], [0, 8]])
                gsv = fap(GS, 0, [[64, NTT], [8, 8], [1, 8]])
                gs2v = fap(GS2, 0, [[64, NTT], [8, 8], [1, 8]])
                tiev = fap(TIE, 0, [[0, NTT], [8, 8], [1, 8]])
                P.inc("dv", v.tensor_tensor(gsv, l_ee, l_e, ALU.is_gt))
                P.inc("dv", v.tensor_tensor(gs2v, l_ee, l_e, ALU.is_equal))
                P.inc("dv", v.tensor_tensor(gs2v, gs2v, tiev, ALU.mult))
                P.inc("dv", v.tensor_tensor(gsv, gsv, gs2v, ALU.add))
                P.inc("dv", v.tensor_reduce(fap(CNT, 0, [[8, NTT], [1, 8], [0, 1]]), gsv,
                                            AX.X, ALU.add))
                P.inc("dv", v.reduce_max(fap(NVM, 0, [[1, NTT], [0, 1]]),
                                         fap(LOG, 0, [[8, NTT], [1, 8]]), AX.X, negate=True))
                P.inc("dv", v.tensor_copy(GS2[:, 0:128], GS[:, 0:128]))
                P.inc("dv", v.tensor_single_scalar(SELB, CNT, 2.0, ALU.is_lt))
                P.inc("dv", v.tensor_tensor(fap(ELB, 0, [[8, NTT], [1, 8]]),
                                            fap(LOG, 0, [[8, NTT], [1, 8]]),
                                            fap(NVM, 0, [[1, NTT], [0, 8]]), ALU.add))

            @blk.scalar
            def _(a):
                W(a, ENGA, dv=P.ctr["dv"])
                P.inc("ac", a.activation(out=ELB, in_=ELB, func=AF.Exp))

            @blk.vector
            def _(v):
                W(v, ENGV, ac=P.ctr["ac"])
                P.inc("dv", v.tensor_tensor(ELB, ELB, SELB, ALU.mult))
                P.inc("dv", v.reduce_sum(fap(SM, 0, [[1, NTT], [0, 1]]),
                                         fap(ELB, 0, [[8, NTT], [1, 8]]), AX.X))
                P.inc("dv", v.tensor_copy(GS2[:, 0:128], GS[:, 0:128]))
                P.inc("dv", v.tensor_copy(GS2[:, 128:256], GS[:, 128:256]))
                P.inc("dv", v.reciprocal(RSM, SM))
                P.inc("dv", v.tensor_copy(GS2[:, 256:384], GS[:, 256:384]))
                P.inc("dv", v.scalar_tensor_tensor(fap(WFT, 0, [[8, NTT], [1, 8]]),
                                                   fap(ELB, 0, [[8, NTT], [1, 8]]), 0.5,
                                                   fap(RSM, 0, [[1, NTT], [0, 8]]),
                                                   ALU.mult, ALU.mult))

            tr_dv = [0] * NTT
            for t in range(NTT):
                tr_mm = 0

                @blk.tensor
                def _(te, t=t):
                    W(te, ENGT, dv=P.ctr["dv"])
                    if t >= 2:
                        W(te, ENGT, dv=tr_dv[t - 2])
                    nonlocal tr_mm
                    tr_mm = P.inc("mm", te.transpose(
                        PS[0:8, 1024 + (t % 2) * 512:1024 + (t % 2) * 512 + 128],
                        WFT[:, t * 8:t * 8 + 8], IDN))

                @blk.vector
                def _(v, t=t, tr_mm_get=lambda: tr_mm):
                    W(v, ENGV, mm=tr_mm_get())
                    tr_dv[t] = P.inc("dv", v.tensor_copy(
                        W8[0:8, t * 128:t * 128 + 128],
                        PS[0:8, 1024 + (t % 2) * 512:1024 + (t % 2) * 512 + 128]))
            return W8

        last_xt_dv = {}

        def emit_ffn(W8):
            WB = SCR[:, 0:2048]
            EWF = [SCR[:, 2048:2048 + 1024], SCR[:, 3072:3072 + 1024]]
            EWH = [BSC[:, 0:1024], BSC[:, 1024:2048]]
            EWL = [BSC[:, 2048:3072], BSC[:, 3072:4096]]
            TMP = [SCR[:, 4096:4096 + 512], SCR[:, 4608:4608 + 512]]
            nidx = NE * DC
            ld_s = [0] * nidx
            split_dv = [0] * nidx
            use_mm = [0] * nidx
            tmp_dv = {}
            wb_dv = [0] * NE
            first = [True]

            def slab_dma(y, idx):
                e, dd = idx // DC, idx % DC
                if first[0]:
                    W(y, ENGY, mm=P.ctr["mm"], dv=P.ctr["dv"], ac=P.ctr["ac"],
                      gp=ln_gp[0])
                    first[0] = False
                if idx >= 2:
                    W(y, ENGY, dv=split_dv[idx - 2])
                ld_s[idx] = P.inc("ld", y.dma_start(
                    out=EWF[idx % 2], in_=ew_d[e, dd].opt()), 16)

            mm_at_entry = P.ctr["mm"]

            def slab_split(idx):
                @blk.vector
                def _(v):
                    W(v, ENGV, ld=ld_s[idx])
                    if idx >= 2:
                        W(v, ENGV, mm=use_mm[idx - 2])
                    else:
                        W(v, ENGV, mm=mm_at_entry)
                    P.inc("dv", v.tensor_copy(EWH[idx % 2], EWF[idx % 2]))
                    split_dv[idx] = P.inc("dv", v.tensor_tensor(
                        EWL[idx % 2], EWF[idx % 2], EWH[idx % 2], ALU.subtract))

            @blk.sync
            def _(y):
                slab_dma(y, 0)
                slab_dma(y, 1)

            slab_split(0)
            slab_split(1)

            for e in range(NE):
                wb_cp = [0] * NMT
                for nq in range(NMT):
                    bc_mm = 0

                    @blk.tensor
                    def _(te, e=e, nq=nq):
                        if nq == 0:
                            W(te, ENGT, dv=wb_dv[e - 1] if e >= 1 else P.ctr["dv"])
                        if nq >= 2:
                            W(te, ENGT, dv=wb_cp[nq - 2])
                        nonlocal bc_mm
                        bc_mm = P.inc("mm", te.matmul(
                            PS[:, (2 + nq % 2) * 512:(2 + nq % 2) * 512 + 512],
                            SELS[:, e * 128:e * 128 + 128],
                            W8[0:8, nq * 512:nq * 512 + 512], start=True, stop=True))

                    @blk.vector
                    def _(v, e=e, nq=nq):
                        W(v, ENGV, mm=bc_mm)
                        wb_cp[nq] = P.inc("dv", v.tensor_copy(
                            WB[:, nq * 512:nq * 512 + 512],
                            PS[:, (2 + nq % 2) * 512:(2 + nq % 2) * 512 + 512]))
                        wb_dv[e] = wb_cp[nq]

                for dd in range(DC):
                    idx = e * DC + dd

                    for mt in range(NMT):
                        @blk.tensor
                        def _(te, idx=idx, e=e, mt=mt):
                            W(te, ENGT, dv=max(split_dv[idx], wb_dv[e]), gp=ln_gp[0])
                            prev = tmp_dv.get((idx, mt - 2), tmp_dv.get((idx - 1, mt + 2)))
                            if prev:
                                W(te, ENGT, dv=prev)
                            for cci in range(DC):
                                for pi, (sta, mov) in enumerate(((EWH, YTH), (EWH, YTL), (EWL, YTH))):
                                    P.inc("mm", te.matmul(
                                        PS[:, (mt % 2) * 512:(mt % 2) * 512 + 512],
                                        sta[idx % 2][:, cci * 128:cci * 128 + 128],
                                        mov[:, cci * T + mt * 512: cci * T + mt * 512 + 512],
                                        start=(cci == 0 and pi == 0),
                                        stop=(cci == DC - 1 and pi == 2)))
                            use_mm[idx] = P.ctr["mm"]

                        @blk.vector
                        def _(v, idx=idx, dd=dd, mt=mt):
                            W(v, ENGV, mm=use_mm[idx])
                            P.inc("dv", v.tensor_tensor(
                                TMP[mt % 2], PS[:, (mt % 2) * 512:(mt % 2) * 512 + 512],
                                WB[:, mt * 512:mt * 512 + 512], ALU.mult))
                            xs = XT[:, dd * T + mt * 512: dd * T + mt * 512 + 512]
                            tmp_dv[(idx, mt)] = P.inc("dv", v.tensor_tensor(xs, xs, TMP[mt % 2], ALU.add))
                            last_xt_dv[(idx, mt)] = tmp_dv[(idx, mt)]

                    @blk.sync
                    def _(y, idx=idx):
                        if idx + 2 < nidx:
                            slab_dma(y, idx + 2)

                    if idx + 2 < nidx:
                        slab_split(idx + 2)

        def emit_qkv():
            WQS = SCR[:, 0:2048]
            WKS = SCR[:, 2048:4096]
            WVS = SCR[:, 4096:12288]
            QKV = SCR[:, 12288:13824]
            SQK = SCR[:, 13824:14336]
            NRM = SCR[:, 14336:14384]
            QHL = BSC[:, 0:1024]
            KHL = BSC[:, 1024:2048]
            ldw = 0

            @blk.sync
            def _(y):
                nonlocal ldw
                W(y, ENGY, mm=P.ctr["mm"], dv=P.ctr["dv"], ac=P.ctr["ac"], gp=ln_gp[0])
                P.inc("ld", y.dma_start(out=WQS, in_=wq_d), 16)
                P.inc("ld", y.dma_start(out=WKS, in_=wk_d), 16)
                ldw = P.inc("ld", y.dma_start(out=WVS, in_=wv_d), 16)

            if dbg:
                @blk.gpsimd
                def _(g):
                    W(g, ENGG, ld=ldw)
                    P.inc("gd", g.dma_start(out=dbgq_d, in_=WQS), 16)

            YSTG = SCR[:, 14384:15408]
            pk_mm = [0] * NTT
            rec_dv = [0] * (NTT + 1)
            ev_dv = [0] * NTT
            st_gd = [0] * NTT

            def emit_rec(t):
                @blk.vector
                def _(v):
                    if t == 0:
                        W(v, ENGV, ld=ldw, gp=ln_gp[0])
                    else:
                        W(v, ENGV, mm=pk_mm[t - 1])
                    yh = fap(YTH, t * 128, [[T, DC], [1, 128]])
                    yl = fap(YTL, t * 128, [[T, DC], [1, 128]])
                    yo = fap(YSTG, 0, [[128, DC], [1, 128]])
                    rec_dv[t] = P.inc("dv", v.tensor_tensor(yo, yh, yl, ALU.add))

            emit_rec(0)
            for t in range(NTT):
                pset = (t % 2) * 4

                @blk.tensor
                def _(te, t=t, pset=pset):
                    W(te, ENGT, ld=ldw, dv=rec_dv[t])
                    if t == 0:
                        W(te, ENGT, dv=P.ctr["dv"])
                    if t >= 2:
                        W(te, ENGT, dv=ev_dv[t - 2])
                    for cci in range(DC):
                        st_, sp_ = cci == 0, cci == DC - 1
                        ysl = YSTG[:, cci * 128: cci * 128 + 128]
                        P.inc("mm", te.matmul(PS[:, pset * 512:pset * 512 + 256], ysl,
                                              WQS[:, cci * 256:cci * 256 + 256], start=st_, stop=sp_))
                        P.inc("mm", te.matmul(PS[:, (pset + 1) * 512:(pset + 1) * 512 + 256], ysl,
                                              WKS[:, cci * 256:cci * 256 + 256], start=st_, stop=sp_))
                        P.inc("mm", te.matmul(PS[:, (pset + 2) * 512:(pset + 2) * 512 + 512], ysl,
                                              WVS[:, cci * 1024:cci * 1024 + 512], start=st_, stop=sp_))
                        P.inc("mm", te.matmul(PS[:, (pset + 3) * 512:(pset + 3) * 512 + 512], ysl,
                                              WVS[:, cci * 1024 + 512:cci * 1024 + 1024], start=st_, stop=sp_))
                    pk_mm[t] = P.ctr["mm"]

                if t + 1 < NTT:
                    emit_rec(t + 1)

                @blk.vector
                def _(v, t=t, pset=pset):
                    W(v, ENGV, mm=pk_mm[t])
                    if t >= 1:
                        W(v, ENGV, gd=st_gd[t - 1])
                    P.inc("dv", v.tensor_copy(QKV[:, 0:256], PS[:, pset * 512:pset * 512 + 256]))
                    P.inc("dv", v.tensor_copy(QKV[:, 256:512], PS[:, (pset + 1) * 512:(pset + 1) * 512 + 256]))
                    P.inc("dv", v.tensor_copy(QKV[:, 512:1024], PS[:, (pset + 2) * 512:(pset + 2) * 512 + 512]))
                    P.inc("dv", v.tensor_copy(QKV[:, 1024:1536], PS[:, (pset + 3) * 512:(pset + 3) * 512 + 512]))

                if dbg and t == NTT - 1:
                    evac3_dv = P.ctr["dv"]

                    @blk.gpsimd
                    def _(g):
                        W(g, ENGG, dv=evac3_dv)
                        P.inc("gd", g.dma_start(out=dbgs_d, in_=QKV), 16)

                for (seg, red_out, red_in) in (
                    (0, fap(NRM, 0, [[1, 32], [0, 1]]), fap(SQK, 0, [[16, 32], [1, 16]])),
                    (1, fap(NRM, 32, [[1, 8], [0, 1]]), fap(SQK, 0, [[64, 8], [1, 64]])),
                    (2, fap(NRM, 40, [[1, 8], [0, 1]]), fap(SQK, 0, [[64, 8], [1, 64]])),
                ):
                    @blk.scalar
                    def _(a, seg=seg):
                        W(a, ENGA, dv=P.ctr["dv"])
                        P.inc("ac", a.activation(out=SQK, in_=QKV[:, seg * 512:(seg + 1) * 512],
                                                 func=AF.Square))

                    @blk.vector
                    def _(v, red_out=red_out, red_in=red_in):
                        W(v, ENGV, ac=P.ctr["ac"])
                        P.inc("dv", v.tensor_reduce(red_out, red_in, AX.X, ALU.add))

                @blk.vector
                def _(v, t=t):
                    P.inc("dv", v.tensor_copy(SQK[:, 0:8], SQK[:, 8:16]))
                    P.inc("dv", v.tensor_single_scalar(NRM, NRM, NRM_EPS * NRM_EPS, ALU.max))

                @blk.scalar
                def _(a):
                    W(a, ENGA, dv=P.ctr["dv"])
                    P.inc("ac", a.activation(out=NRM, in_=NRM, func=AF.Sqrt))

                @blk.vector
                def _(v, t=t):
                    W(v, ENGV, ac=P.ctr["ac"])
                    P.inc("dv", v.reciprocal(NRM, NRM))
                    qv = fap(QKV, 0, [[16, 16], [1, 16]])
                    kv_ = fap(QKV, 256, [[16, 16], [1, 16]])
                    vv = fap(QKV, 512, [[64, 16], [1, 64]])
                    P.inc("dv", v.tensor_tensor(qv, qv, fap(NRM, 0, [[1, 16], [0, 16]]), ALU.mult))
                    P.inc("dv", v.tensor_tensor(kv_, kv_, fap(NRM, 16, [[1, 16], [0, 16]]), ALU.mult))
                    P.inc("dv", v.tensor_tensor(vv, vv, fap(NRM, 32, [[1, 16], [0, 64]]), ALU.mult))
                    qh0 = fap(QHL, 0, [[64, 16], [1, 16]])
                    ql1 = fap(QHL, 16, [[64, 16], [1, 16]])
                    qh2 = fap(QHL, 32, [[64, 16], [1, 16]])
                    ql3 = fap(QHL, 48, [[64, 16], [1, 16]])
                    P.inc("dv", v.tensor_copy(qh0, qv))
                    P.inc("dv", v.tensor_tensor(ql1, qv, qh0, ALU.subtract))
                    P.inc("dv", v.tensor_copy(qh2, qh0))
                    P.inc("dv", v.tensor_copy(ql3, ql1))
                    kh0 = fap(KHL, 0, [[64, 16], [1, 16]])
                    kl1 = fap(KHL, 16, [[64, 16], [1, 16]])
                    kl2 = fap(KHL, 32, [[64, 16], [1, 16]])
                    kh3 = fap(KHL, 48, [[64, 16], [1, 16]])
                    P.inc("dv", v.tensor_copy(kh0, kv_))
                    P.inc("dv", v.tensor_tensor(kl1, kv_, kh0, ALU.subtract))
                    P.inc("dv", v.tensor_copy(kl2, kl1))
                    ev_dv[t] = P.inc("dv", v.tensor_copy(kh3, kh0))

                @blk.gpsimd
                def _(g, t=t):
                    W(g, ENGG, dv=ev_dv[t])
                    P.inc("gd", g.dma_start(out=qhl_d[t * 128:(t + 1) * 128, :], in_=QHL), 16)
                    P.inc("gd", g.dma_start(out=khl_d[t * 128:(t + 1) * 128, :], in_=KHL), 16)
                    st_gd[t] = P.inc("gd", g.dma_start(out=vht_d[t * 128:(t + 1) * 128, :],
                                                       in_=QKV[:, 512:1536]), 16)
                    if t in (NTT // 2 - 1, NTT - 1):
                        c = 0 if t == NTT // 2 - 1 else 1
                        W(g, ENGG, gd=st_gd[t])
                        r0, r1 = c * (T // 2), (c + 1) * (T // 2)
                        P.inc("cc", g.collective_compute(
                            "AllGather", ALU.bypass, replica_groups=[[0, 1, 2, 3], [4, 5, 6, 7]],
                            ins=[khl_d[r0:r1, :].opt()], outs=[khl_all[c].opt()]))
                        cc_chunk[c] = P.inc("cc", g.collective_compute(
                            "AllGather", ALU.bypass, replica_groups=[[0, 1, 2, 3], [4, 5, 6, 7]],
                            ins=[vht_d[r0:r1, :].opt()], outs=[vht_all[c].opt()]))
            return [st_gd[NTT // 2 - 1], st_gd[NTT - 1]]

        outsrc = XT
        ln_gp = [0]
        if stage == "ln1":
            ln_gp[0] = emit_ln()
            outsrc = XT
        elif stage == "gate1":
            ln_gp[0] = emit_ln()
            W8a = emit_gating()

            @blk.gpsimd
            def _(g):
                W(g, ENGG, dv=P.ctr["dv"], gp=ln_gp[0])
                P.inc("gd", g.dma_start(out=dbgw_d, in_=SCR[0:8, 9728:9728 + T]), 16)
                P.inc("gd", g.dma_start(out=dbgl_d, in_=LOG), 16)
                P.inc("gd", g.dma_start(out=dbgf_d, in_=WFT), 16)
                P.inc("gd", g.dma_start(out=dap(khl_dbg, 0, [[16384, 128], [1, 16384]]),
                                        in_=YTH), 16)
                P.inc("gd", g.dma_start(out=dap(vht_dbg, 0, [[16384, 128], [1, 16384]]),
                                        in_=YTL), 16)
            outsrc = XT
        elif stage != "xio":
            ln_gp[0] = emit_ln()
            W8a = emit_gating()
            emit_ffn(W8a)
            if dbg and stage == "ffn1":
                @blk.gpsimd
                def _(g):
                    W(g, ENGG, dv=P.ctr["dv"])
                    P.inc("gd", g.dma_start(out=dbgw_d, in_=SCR[0:8, 9728:9728 + T]), 16)
                    P.inc("gd", g.dma_start(out=dbgl_d, in_=LOG), 16)
                    P.inc("gd", g.dma_start(out=dbgf_d, in_=WFT), 16)

        qkv_gd_half = [0, 0]
        cc_chunk = [0, 0]
        if stage in ("qkv", "attn", "full"):
            ln_gp[0] = emit_ln()
            qkv_gd_half = emit_qkv()

        @blk.gpsimd
        def _(g):
            if stage not in ("qkv", "attn", "full"):
                return
            if dbg:
                W(g, ENGG, cc=cc_chunk[1])
                for c in range(2):
                    P.inc("gd", g.dma_start(out=khl_dbg[c * (NK // 2):(c + 1) * (NK // 2), :],
                                            in_=khl_all[c]), 16)
                    P.inc("gd", g.dma_start(out=vht_dbg[c * (NK // 2):(c + 1) * (NK // 2), :],
                                            in_=vht_all[c]), 16)
                P.inc("gd", g.dma_start(out=qhl_dbg, in_=qhl_d), 16)

        VSL = [SCR[:, 0:1024], SCR[:, 1024:2048]]
        A2B = [SCR[:, 4096:5120], SCR[:, 5120:6144], SCR[:, 7168:8192]]
        R2B = [SCR[:, 2048:3072], SCR[:, 3072:4096], SCR[:, 6144:7168]]
        OSS = SCR[:, 8192:10240]
        SQ2 = SCR[:, 10240:10752]
        NR2 = SCR[0:2, 10752:10752 + T]
        THR = SCR[0:2, 12800:12800 + T]
        QST = [BSC[:, 0:2048], BSC[:, 2048:4096]]
        KST = [BSC[:, 4096:5120], BSC[:, 5120:6144]]

        qst_ld = [0] * HPAIR
        kst_ld = {}
        vsl_ld = {}
        s_mm = {}
        relu_ac = {}
        mult_dv = {}
        av_mm = {}
        av_mmj = {}
        pair_done_dv = [0] * HPAIR
        oss_dvd = {}
        sq_acd = {}
        n2_mmd = {}
        sqrt_acd = {}
        tanh_acd = {}
        scale_dv = {}
        sq2m_dv = {}

        attn_start_dv = P.ctr["dv"]
        run_attn = stage in ("attn", "full")
        jmeta = {}

        def emit_s(p, i, sl, kc, qt, j):
            jmeta[(p, j)] = (i, sl, kc, qt)
            sb = j % 2

            @blk.tensor
            def _(te):
                W(te, ENGT, ld=max(kst_ld[(p, sl)], qst_ld[p]))
                if j >= 2:
                    W(te, ENGT, dv=mult_dv[(p, j - 2)])
                elif p >= 1:
                    W(te, ENGT, dv=mult_dv[(p - 1, NJ - 2 + j)])
                else:
                    W(te, ENGT, dv=attn_start_dv)
                b0 = (4 + sb * 2) * 512
                P.inc("mm", te.matmul(PS[:, b0:b0 + 512],
                                      KST[i % 2][0:64, kc * 128:kc * 128 + 128],
                                      QST[p % 2][0:64, qt * 512:qt * 512 + 512],
                                      start=True, stop=True))
                s_mm[(p, sl, kc, qt)] = P.inc("mm", te.matmul(
                    PS[:, b0 + 512:b0 + 1024],
                    KST[i % 2][64:128, kc * 128:kc * 128 + 128],
                    QST[p % 2][64:128, qt * 512:qt * 512 + 512],
                    start=True, stop=True))

        def emit_evac(p, j):
            i, sl, kc, qt = jmeta[(p, j)]
            sb2 = j % 2
            sb3 = j % 3
            b0 = (4 + sb2 * 2) * 512
            on_gp = (j % 3 != 2)

            @blk.vector
            def _(v):
                W(v, ENGV, mm=s_mm[(p, sl, kc, qt)])
                jj = j - 3
                pp = p
                if jj < 0 and p >= 1:
                    pp, jj = p - 1, NJ + jj
                if jj >= 0:
                    pk = ("gp" if jj % 3 != 2 else "ac")
                    W(v, ENGV, **{pk: relu_ac[(pp, jj)]})
                mult_dv[(p, j)] = P.inc("dv", v.tensor_scalar(
                    R2B[sb3], PS[:, b0:b0 + 1024], 0.0, None, ALU.max))

            def sq_waits(eng, ename):
                W(eng, ename, dv=mult_dv[(p, j)])
                jj = j - 3
                pp = p
                if jj < 0 and p >= 1:
                    pp, jj = p - 1, NJ + jj
                if jj >= 0:
                    W(eng, ename, mm=av_mmj[(pp, jj)])

            if on_gp:
                @blk.gpsimd
                def _(g):
                    sq_waits(g, ENGG)
                    relu_ac[(p, j)] = P.inc("gp", g.tensor_tensor(
                        A2B[sb3], R2B[sb3], R2B[sb3], ALU.mult))
            else:
                @blk.scalar
                def _(a):
                    sq_waits(a, ENGA)
                    relu_ac[(p, j)] = P.inc("ac", a.activation(
                        out=A2B[sb3], in_=R2B[sb3], func=AF.Square))

        def emit_av(p, j):
            i, sl, kc, qt = jmeta[(p, j)]
            sb3 = j % 3
            sq_sem = "gp" if (j % 3 != 2) else "ac"

            @blk.tensor
            def _(te):
                W(te, ENGT, ld=vsl_ld[(p, sl)], **{sq_sem: relu_ac[(p, j)]})
                st_ = (j < NQT)
                sp_ = (j >= NJ - NQT)
                if st_ and p >= 1:
                    W(te, ENGT, dv=sq2m_dv[(p - 1, qt)])
                vb = kc * 128
                P.inc("mm", te.matmul(PS[0:64, qt * 512:qt * 512 + 512],
                                      VSL[i % 2][:, vb:vb + 64], A2B[sb3][:, 0:512],
                                      start=st_, stop=sp_, tile_position=(0, 0)))
                av_mm[(p, sl, kc, qt)] = P.inc("mm", te.matmul(
                    PS[64:128, qt * 512:qt * 512 + 512],
                    VSL[i % 2][:, vb + 64:vb + 128], A2B[sb3][:, 512:1024],
                    start=st_, stop=sp_, tile_position=(0, 64)))
                av_mmj[(p, j)] = av_mm[(p, sl, kc, qt)]

        def emit_norm_head_a(p, qt):
            @blk.vector
            def _(v):
                W(v, ENGV, mm=av_mm[(p, NSLAB - 1, NKC - 1, qt)])
                oss_dvd[(p, qt)] = P.inc("dv", v.tensor_copy(
                    OSS[:, qt * 512:qt * 512 + 512], PS[:, qt * 512:qt * 512 + 512]))

            @blk.scalar
            def _(a):
                W(a, ENGA, dv=oss_dvd[(p, qt)])
                if qt >= 1:
                    W(a, ENGA, mm=n2_mmd[(p, qt - 1)])
                elif p >= 1:
                    W(a, ENGA, mm=n2_mmd[(p - 1, NQT - 1)])
                sq_acd[(p, qt)] = P.inc("ac", a.activation(
                    out=SQ2, in_=OSS[:, qt * 512:qt * 512 + 512], func=AF.Square))

            @blk.tensor
            def _(te):
                W(te, ENGT, ac=sq_acd[(p, qt)])
                n2_mmd[(p, qt)] = P.inc("mm", te.matmul(
                    PS[0:2, qt * 512:qt * 512 + 512], BLK2, SQ2, start=True, stop=True))

        def emit_norm_head_b1(p, qt):
            nr_dv = [0]

            @blk.vector
            def _(v):
                W(v, ENGV, mm=n2_mmd[(p, qt)])
                nr_dv[0] = P.inc("dv", v.tensor_copy(
                    NR2[0:2, qt * 512:qt * 512 + 512], PS[0:2, qt * 512:qt * 512 + 512]))

            @blk.scalar
            def _(a):
                W(a, ENGA, dv=nr_dv[0])
                sqrt_acd[(p, qt)] = P.inc("ac", a.activation(
                    out=NR2[0:2, qt * 512:qt * 512 + 512],
                    in_=NR2[0:2, qt * 512:qt * 512 + 512], func=AF.Sqrt))

        def emit_norm_head_b2(p, qt):
            thr_dv = [0]

            @blk.vector
            def _(v):
                W(v, ENGV, ac=sqrt_acd[(p, qt)])
                P.inc("dv", v.tensor_single_scalar(
                    THR[0:2, qt * 512:qt * 512 + 512],
                    NR2[0:2, qt * 512:qt * 512 + 512], NRM_EPS, ALU.max))
                thr_dv[0] = P.inc("dv", v.reciprocal(
                    THR[0:2, qt * 512:qt * 512 + 512],
                    THR[0:2, qt * 512:qt * 512 + 512]))

            @blk.scalar
            def _(a):
                W(a, ENGA, dv=thr_dv[0])
                tanh_acd[(p, qt)] = P.inc("ac", a.activation(
                    out=NR2[0:2, qt * 512:qt * 512 + 512],
                    in_=NR2[0:2, qt * 512:qt * 512 + 512], func=AF.Tanh))

        def emit_norm_tail(p, qt):
            @blk.vector
            def _(v):
                W(v, ENGV, ac=tanh_acd[(p, qt)])
                scale_dv[(p, qt)] = P.inc("dv", v.tensor_tensor(
                    NR2[0:2, qt * 512:qt * 512 + 512],
                    NR2[0:2, qt * 512:qt * 512 + 512],
                    THR[0:2, qt * 512:qt * 512 + 512], ALU.mult))

            bq_mm = [0]

            @blk.tensor
            def _(te):
                W(te, ENGT, dv=scale_dv[(p, qt)])
                bq_mm[0] = P.inc("mm", te.matmul(
                    PS[:, qt * 512:qt * 512 + 512], SEL2,
                    NR2[0:2, qt * 512:qt * 512 + 512], start=True, stop=True))

            @blk.vector
            def _(v):
                W(v, ENGV, mm=bq_mm[0])
                o = OSS[:, qt * 512:qt * 512 + 512]
                sq2m_dv[(p, qt)] = P.inc("dv", v.tensor_tensor(
                    o, o, PS[:, qt * 512:qt * 512 + 512], ALU.mult))
                ath = YTH[:, p * T + qt * 512: p * T + qt * 512 + 512]
                atl = YTL[:, p * T + qt * 512: p * T + qt * 512 + 512]
                P.inc("dv", v.tensor_copy(ath, o))
                pair_done_dv[p] = P.inc("dv", v.tensor_tensor(atl, o, ath, ALU.subtract))

        @blk.sync
        def _(y):
            if not run_attn:
                return
            W(y, ENGY, gd=P.ctr["gd"], mm=P.ctr["mm"], dv=P.ctr["dv"])
            qst_ld[0] = P.inc("ld", y.dma_start(out=QST[0], in_=qhl_d[:, 0:128], transpose=True), 16)

        for p in (range(HPAIR) if run_attn else ()):
            @blk.sync
            def _(y, p=p):
                W(y, ENGY, cc=cc_chunk[0])
                if p + 1 < HPAIR:
                    if p >= 1:
                        W(y, ENGY, mm=s_mm[(p - 1, NSLAB - 1, NKC - 1, NQT - 1)])
                    qst_ld[p + 1] = P.inc("ld", y.dma_start(
                        out=QST[(p + 1) % 2], in_=qhl_d[:, (p + 1) * 128:(p + 2) * 128],
                        transpose=True), 16)

            for sidx, sl in enumerate((0, 2, 4, 6, 1, 3, 5, 7)):
                i = p * NSLAB + sidx

                @blk.sync
                def _(y, p=p, sl=sl, i=i):
                    if i >= 2:
                        order = (0, 2, 4, 6, 1, 3, 5, 7)
                        pi, si = (i - 2) // NSLAB, order[(i - 2) % NSLAB]
                        W(y, ENGY, mm=av_mm[(pi, si, NKC - 1, NQT - 1)])
                    ch, rk = sl % 2, sl // 2
                    W(y, ENGY, cc=cc_chunk[ch])
                    kst_ld[(p, sl)] = P.inc("ld", y.dma_start(
                        out=KST[i % 2],
                        in_=khl_all[ch][rk * 1024:(rk + 1) * 1024, p * 128:(p + 1) * 128],
                        transpose=True), 16)
                    src = dap(vht_all[ch], (rk * 1024) * D + p * 128, [[D, 128], [128 * D, 8], [1, 128]])
                    vsl_ld[(p, sl)] = P.inc("ld", y.dma_start(
                        out=fap(VSL[i % 2], 0, [[128, 8], [1, 128]]), in_=src), 16)

                for kc in range(NKC):
                    for qt in range(NQT):
                        j = (sidx * NKC + kc) * NQT + qt
                        emit_s(p, i, sl, kc, qt, j)
                        if j >= 1:
                            emit_evac(p, j - 1)
                        if p >= 1 and 3 <= j <= NQT + 2:
                            emit_norm_tail(p - 1, j - 3)
                        if j >= 3:
                            emit_av(p, j - 3)
                        if p >= 1:
                            if 2 <= j <= NQT + 1:
                                emit_norm_head_b2(p - 1, j - 2)
                            if 1 <= j <= NQT:
                                emit_norm_head_b1(p - 1, j - 1)
                            if j < NQT:
                                emit_norm_head_a(p - 1, j)
                if sidx == NSLAB - 1:
                    emit_evac(p, NJ - 1)
                    emit_av(p, NJ - 3)
                    emit_av(p, NJ - 2)
                    emit_av(p, NJ - 1)

            pass

        if run_attn:
            for qt in range(NQT):
                emit_norm_head_a(HPAIR - 1, qt)
            for qt in range(NQT):
                emit_norm_head_b1(HPAIR - 1, qt)
            for qt in range(NQT):
                emit_norm_head_b2(HPAIR - 1, qt)
            for qt in range(NQT):
                emit_norm_tail(HPAIR - 1, qt)

        if dbg and stage == "attn":
            @blk.gpsimd
            def _(g):
                W(g, ENGG, dv=pair_done_dv[HPAIR - 1])
                P.inc("gd", g.dma_start(out=att_dbg, in_=YTH), 16)

        WOS = SCR[:, 0:8192]
        wo_ld = 0

        @blk.sync
        def _(y):
            nonlocal wo_ld
            if not run_attn:
                return
            W(y, ENGY, dv=P.ctr["dv"], mm=P.ctr["mm"], ac=P.ctr["ac"])
            wo_ld = P.inc("ld", y.dma_start(out=WOS, in_=wo_d), 16)

        wo_add_dv = {}
        Y32 = SCR[:, 8192:12288]
        rec32_dv = [0] * NMT
        for mt in (range(NMT) if run_attn else ()):
            @blk.vector
            def _(v, mt=mt):
                if mt == 0:
                    W(v, ENGV, ld=wo_ld)
                    W(v, ENGV, dv=pair_done_dv[HPAIR - 1])
                if mt >= 1:
                    W(v, ENGV, mm=P.ctr["mm"])
                for cci in range(DC):
                    yh = YTH[:, cci * T + mt * 512: cci * T + mt * 512 + 512]
                    yl = YTL[:, cci * T + mt * 512: cci * T + mt * 512 + 512]
                    P.inc("dv", v.tensor_tensor(Y32[:, cci * 512:cci * 512 + 512], yh, yl, ALU.add))
                rec32_dv[mt] = P.ctr["dv"]

            for dd in range(DC):
                k = mt * DC + dd
                wo_mm = 0

                @blk.tensor
                def _(te, dd=dd, mt=mt, k=k):
                    W(te, ENGT, ld=wo_ld, dv=rec32_dv[mt])
                    if k >= 2:
                        W(te, ENGT, dv=wo_add_dv[k - 2])
                    for cci in range(DC):
                        P.inc("mm", te.matmul(PS[:, (k % 2) * 512:(k % 2) * 512 + 512],
                                              WOS[:, cci * 1024 + dd * 128: cci * 1024 + dd * 128 + 128],
                                              Y32[:, cci * 512:cci * 512 + 512],
                                              start=(cci == 0), stop=(cci == DC - 1)))
                    nonlocal wo_mm
                    wo_mm = P.ctr["mm"]

                @blk.vector
                def _(v, dd=dd, mt=mt, k=k):
                    W(v, ENGV, mm=wo_mm)
                    xs = XT[:, dd * T + mt * 512: dd * T + mt * 512 + 512]
                    wo_add_dv[k] = P.inc("dv", v.tensor_tensor(
                        xs, xs, PS[:, (k % 2) * 512:(k % 2) * 512 + 512], ALU.add))

        if stage == "full":
            ln_gp[0] = emit_ln()
            W8b = emit_gating()
            emit_ffn(W8b)

        ot_dv = [0] * NTT
        ot_gd = [0] * NTT
        for t in range(NTT):
            ot_mm = 0

            @blk.tensor
            def _(te, t=t):
                W(te, ENGT, dv=last_xt_dv.get((NE * DC - 1, t // NMT), P.ctr["dv"]))
                if t >= 2:
                    W(te, ENGT, dv=ot_dv[t - 2])
                g = (t % 2) * 2 + 4
                for c in range(DC):
                    P.inc("mm", te.transpose(PS[:, g * 512 + c * 128: g * 512 + c * 128 + 128],
                                             outsrc[:, c * T + t * 128: c * T + t * 128 + 128], IDN))
                nonlocal ot_mm
                ot_mm = P.ctr["mm"]

            @blk.vector
            def _(v, t=t):
                W(v, ENGV, mm=ot_mm)
                if t >= 2:
                    W(v, ENGV, gd=ot_gd[t - 2])
                g = (t % 2) * 2 + 4
                src = fap(PS, g * 512, [[128, 8], [1, 128]])
                ot_dv[t] = P.inc("dv", v.tensor_copy(SCR[:, (t % 2) * 1024:(t % 2) * 1024 + 1024], src))

            @blk.gpsimd
            def _(g_, t=t):
                W(g_, ENGG, dv=ot_dv[t])
                ot_gd[t] = P.inc("gd", g_.dma_start(out=out_d[t * 128:(t + 1) * 128, :],
                                                    in_=SCR[:, (t % 2) * 1024:(t % 2) * 1024 + 1024]), 16)

        @blk.gpsimd
        def _(g_):
            W(g_, ENGG, gd=P.ctr["gd"])

    return nc


_PROG = None


def kernel(L0_attn_tokens, L0_attn_Wq, L0_attn_Wkv, L0_attn_Wout, L0_ffn_gate_w, L0_ffn_expert_w):
    global _PROG
    x = np.ascontiguousarray(np.asarray(L0_attn_tokens, dtype=np.float32))

    def sbimg(w):
        n = w.shape[1]
        return np.ascontiguousarray(w.reshape(8, 128, n).transpose(1, 0, 2).reshape(128, 8 * n))

    wq = sbimg(np.asarray(L0_attn_Wq, dtype=np.float32))
    wkv = np.asarray(L0_attn_Wkv, dtype=np.float32)
    cols = np.arange(1280).reshape(16, 80)
    wk = sbimg(np.ascontiguousarray(wkv[:, cols[:, :16].reshape(-1)]))
    wv = sbimg(np.ascontiguousarray(wkv[:, cols[:, 16:].reshape(-1)]))
    wo = sbimg(np.ascontiguousarray(np.asarray(L0_attn_Wout, dtype=np.float32)))
    gt = sbimg(np.ascontiguousarray(np.asarray(L0_ffn_gate_w, dtype=np.float32).T))
    ew_t = np.asarray(L0_ffn_expert_w, dtype=np.float32).transpose(0, 2, 1)
    ew = np.ascontiguousarray(
        ew_t.reshape(8, 8, 128, 8, 128).transpose(0, 3, 2, 1, 4).reshape(8, 8, 128, 1024))

    if _PROG is None:
        _PROG = build()
    nc = _PROG

    in_maps = []
    for c in range(8):
        b, tc = c // 4, c % 4
        xs = np.ascontiguousarray(x[b, tc * T:(tc + 1) * T, :])
        in_maps.append({"x": xs, "wq": wq, "wk": wk, "wv": wv, "wo": wo, "gt": gt, "ew": ew})
    res = run_bass_kernel_spmd(nc, in_maps, core_ids=list(range(8)))
    out = np.empty((2, 8192, 1024), np.float32)
    for c in range(8):
        b, tc = c // 4, c % 4
        out[b, tc * T:(tc + 1) * T, :] = res.results[c]["out"]
    return out
